# version 09
# baseline (speedup 1.0000x reference)
import numpy as np
import ml_dtypes
from contextlib import ExitStack

import concourse.bass as bass
import concourse.mybir as mybir
from concourse.bass_utils import run_bass_kernel_spmd

F32 = mybir.dt.float32
BF16 = mybir.dt.bfloat16
U32 = mybir.dt.uint32
AF = mybir.ActivationFunctionType
ALU = mybir.AluOpType
AX = mybir.AxisListType

T, D = 4096, 1024
NTT = T // 128
NCH = T // 512
NH, HD, DC, DR = 16, 64, 128, 32
M, NE, TOPK = 2, 8, 2
FF = 2048
EPS_OUT, EPS_IN = 1e-5, 1e-6
SC_FFN = 1.0 / 8.0
SC_MLA = 1.0 / float(np.sqrt(96.0))

import os as _os
_NOBC = _os.environ.get("KNOBC", "0") == "1"
_NOVB = _os.environ.get("KNOVB", "0") == "1"
_CUT = None
_DBG = {}


class _TV:
    def __init__(self, h, p0, np_, c0, w):
        self.h, self.p0, self.np_, self.c0, self.w = h, p0, np_, c0, w

    def __getitem__(self, idx):
        if not isinstance(idx, tuple):
            idx = (slice(None), slice(None))
        rs, cs = idx
        r0 = rs.start or 0
        r1 = rs.stop if rs.stop is not None else self.np_
        a = cs.start or 0
        bb = cs.stop if cs.stop is not None else self.w
        return self.h[self.p0 + r0: self.p0 + r1, self.c0 + a: self.c0 + bb]


class _RV:
    def __init__(self, h, p0, c0, w):
        self.h, self.p0, self.c0, self.w = h, p0, c0, w

    def __getitem__(self, idx):
        rows, cols = idx
        assert rows.start in (0, None) and rows.stop in (1, None)
        a = cols.start or 0
        bb = cols.stop if cols.stop is not None else self.w
        return self.h[self.p0:self.p0 + 1, self.c0 + a: self.c0 + bb]


class Buf:
    __slots__ = ("name", "w", "r")

    def __init__(self, name):
        self.name = name
        self.w = {}
        self.r = {}


class KB:
    ENG = ("pe", "act", "dve", "pool", "sp")

    def __init__(self, nc, stk):
        self.nc = nc
        self.stk = stk
        self._nsem = 0
        self.ops = {e: [] for e in self.ENG}
        self.sem_cur = {}
        self.sem_val = {}
        for e in self.ENG:
            self._rotate_engine_sem(e)
        self.sp_ring = [self._mksem(f"D{i}") for i in range(24)]
        self.sp_cnt = [0] * 24
        self.sp_i = 0
        self.pool_ring = [self._mksem(f"P{i}") for i in range(28)]
        self.pool_cnt = [0] * 28
        self.pool_i = 0
        self.sc_ring = [self._mksem(f"C{i}") for i in range(6)]
        self.sc_cnt = [0] * 6
        self.sc_i = 0
        self.wm = {e: {} for e in self.ENG}
        self._nsem = 0

    def _mksem(self, name):
        self._nsem += 1
        return self.stk.enter_context(self.nc.semaphore(name))

    def _rotate_engine_sem(self, e):
        self.sem_cur[e] = self._mksem(f"S{e}{self.sem_val.get(e, 0)}x")
        self.sem_val[e] = 0

    def _wait(self, e, tok):
        sem, val = tok
        wm = self.wm[e]
        k = id(sem)
        if wm.get(k, 0) >= val:
            return
        wm[k] = val
        self.ops[e].append(lambda h, s=sem, v=val: h.wait_ge(s, v))

    def _put(self, d, tok):
        k = id(tok[0])
        old = d.get(k)
        if old is None or old[1] < tok[1]:
            d[k] = tok

    def _deps(self, e, reads, writes):
        for b in reads:
            for tok in b.w.values():
                self._wait(e, tok)
        for b in writes:
            for tok in b.w.values():
                self._wait(e, tok)
            for tok in b.r.values():
                self._wait(e, tok)

    def _record(self, tok, reads, writes):
        for b in reads:
            self._put(b.r, tok)
        for b in writes:
            self._put(b.w, tok)
            b.r = {}

    def op(self, e, fn, reads=(), writes=()):
        self._deps(e, reads, writes)
        if self.sem_val[e] >= 30000:
            self._rotate_engine_sem(e)
        sem = self.sem_cur[e]
        self.sem_val[e] += 1
        val = self.sem_val[e]
        self.ops[e].append(lambda h, f=fn, s=sem: f(h).then_inc(s, 1))
        tok = (sem, val)
        self._record(tok, reads, writes)
        return tok

    def op_group(self, e, fns, reads=(), writes=()):
        self._deps(e, reads, writes)
        if self.sem_val[e] >= 30000:
            self._rotate_engine_sem(e)
        for fn in fns[:-1]:
            self.ops[e].append(lambda h, f=fn: f(h))
        sem = self.sem_cur[e]
        self.sem_val[e] += 1
        val = self.sem_val[e]
        self.ops[e].append(lambda h, f=fns[-1], s=sem: f(h).then_inc(s, 1))
        tok = (sem, val)
        self._record(tok, reads, writes)
        return tok

    def dma(self, out_ap, in_ap, reads=(), writes=(), q="sp"):
        if q == "sc":
            e, ring, cnt = "act", self.sc_ring, self.sc_cnt
            i = self.sc_i
            self.sc_i = (self.sc_i + 1) % len(ring)
        else:
            e, ring, cnt = "sp", self.sp_ring, self.sp_cnt
            i = self.sp_i
            self.sp_i = (self.sp_i + 1) % len(ring)
        self._deps(e, reads, writes)
        sem = ring[i]
        if cnt[i] > 0:
            self._wait(e, (sem, 16 * cnt[i]))
        cnt[i] += 1
        val = 16 * cnt[i]
        self.ops[e].append(
            lambda h, o=out_ap, ii=in_ap, s=sem: h.dma_start(out=o, in_=ii).then_inc(s, 16))
        tok = (sem, val)
        self._record(tok, reads, writes)
        return tok

    def pool_dma(self, out_ap, in_ap, reads=(), writes=(), indirect=None):
        self._deps("pool", reads, writes)
        i = self.pool_i
        self.pool_i = (self.pool_i + 1) % len(self.pool_ring)
        sem = self.pool_ring[i]
        if self.pool_cnt[i] > 0:
            self._wait("pool", (sem, 16 * self.pool_cnt[i]))
        self.pool_cnt[i] += 1
        val = 16 * self.pool_cnt[i]
        if indirect is None:
            self.ops["pool"].append(
                lambda h, o=out_ap, ii=in_ap, s=sem: h.dma_start(out=o, in_=ii).then_inc(s, 16))
        else:
            self.ops["pool"].append(
                lambda h, o=out_ap, ii=in_ap, s=sem, off=indirect:
                    h.indirect_dma_start(out=o, out_offset=None, in_=ii, in_offset=off)
                    .then_inc(s, 16))
        tok = (sem, val)
        self._record(tok, reads, writes)
        return tok

    def wm_clear(self, sem):
        k = id(sem)
        for e in self.ENG:
            self.wm[e].pop(k, None)

    def barrier_wait(self, e, toks):
        for t in toks:
            if t is not None:
                self._wait(e, t)

    def alias(self, newbuf, oldbufs):
        for ob in oldbufs:
            if ob is newbuf:
                continue
            for tok in ob.w.values():
                self._put(newbuf.r, tok)
            for tok in ob.r.values():
                self._put(newbuf.r, tok)

    def finalize(self, block):
        for e, attr in (("sp", "sync"), ("pe", "tensor"), ("act", "scalar"),
                        ("dve", "vector"), ("pool", "gpsimd")):
            ops = self.ops[e]
            dec = getattr(block, attr)

            def mk(ops_list):
                def body(h):
                    for f in ops_list:
                        f(h)
                return body
            dec(mk(ops))


def _hostpack(inputs):
    f32 = np.float32
    g = lambda x: np.ascontiguousarray(np.asarray(x, dtype=f32))
    d = {}
    d["wkv"] = g(inputs["L0_attn_W_kv_down"])
    d["wkup"] = g(inputs["L0_attn_W_k_up"])
    d["wvup"] = g(inputs["L0_attn_W_v_up"])
    d["wqa"] = g(inputs["L0_attn_W_q"])
    wqr = g(inputs["L0_attn_W_q_rope"])
    wkr = g(inputs["L0_attn_W_k_rope"])
    wkr48 = np.zeros((128, 48), np.float32)
    wkr48[:, 0:16] = wkr[:, 0:16]; wkr48[:, 32:48] = wkr[:, 16:32]
    d["wkr"] = wkr48
    wqr48 = np.zeros((1024, 48), np.float32)
    wqr48[:, 0:16] = wqr[:, 0:16]; wqr48[:, 32:48] = wqr[:, 16:32]
    d["wqr"] = wqr48
    d["woa"] = g(inputs["L0_attn_W_o"])
    for nm, key in (("g1", "L0_ffn_ln1_g"), ("b1", "L0_ffn_ln1_b"),
                    ("g2", "L0_ffn_ln2_g"), ("b2", "L0_ffn_ln2_b")):
        d[nm] = g(inputs[key]).reshape(8, 128).T.copy()
    d["wq"] = g(inputs["L0_ffn_mha_wq"]).reshape(M, D, NH * HD)
    d["wk"] = g(inputs["L0_ffn_mha_wk"]).reshape(M, D, NH * HD)
    d["wv"] = g(inputs["L0_ffn_mha_wv"]).reshape(M, D, NH * HD)
    d["wo"] = g(inputs["L0_ffn_mha_wo"]).reshape(M, NH * HD, D)
    bq = g(inputs["L0_ffn_mha_bq"]).reshape(M, NH * HD)
    bk = g(inputs["L0_ffn_mha_bk"]).reshape(M, NH * HD)
    d["bqc"] = np.concatenate([bq[m].reshape(8, 128).T for m in range(M)], axis=1).copy()
    d["bkc"] = np.concatenate([bk[m].reshape(8, 128).T for m in range(M)], axis=1).copy()
    d["bv"] = g(inputs["L0_ffn_mha_bv"]).reshape(M, NH * HD)
    d["bo"] = g(inputs["L0_ffn_mha_bo"])
    d["mrw"] = g(inputs["L0_ffn_mha_rw"]).reshape(8, 128, M).transpose(1, 0, 2).reshape(128, 8 * M).copy()
    d["mrb"] = g(inputs["L0_ffn_mha_rb"]).reshape(1, M)
    d["frw"] = g(inputs["L0_ffn_ffn_rw"]).reshape(8, 128, NE).transpose(1, 0, 2).reshape(128, 8 * NE).copy()
    d["frb"] = g(inputs["L0_ffn_ffn_rb"]).reshape(1, NE)
    d["fw1"] = g(inputs["L0_ffn_fw1"]).reshape(NE * D, FF)
    d["fw2"] = g(inputs["L0_ffn_fw2"]).reshape(NE * FF, D)
    d["ident"] = np.eye(128, dtype=f32).astype(ml_dtypes.bfloat16)
    inv_freq = 1.0 / 10000.0 ** (np.arange(0, DR, 2, dtype=np.float64) / DR)
    tt = np.arange(T, dtype=np.float64)
    fr = np.outer(tt, inv_freq)
    d["cosT"] = np.cos(fr).T.astype(f32).copy()
    d["sinT"] = np.sin(fr).T.astype(f32).copy()
    mk = (np.arange(896)[None, :] >= (384 + np.arange(128)[:, None])).astype(f32)
    d["dmask"] = mk.astype(ml_dtypes.bfloat16)
    return d


def build_program():
    nc = bass.Bass("TRN2", target_bir_lowering=False, debug=False)
    stk = ExitStack()
    with stk:
        kb = KB(nc, stk)
        mg = _Mega(nc, kb, stk)
        mg.declare()
        mg.emit()
        block = stk.enter_context(nc.Block())
        kb.finalize(block)
    return nc


class _Mega:
    def __init__(self, nc, kb, stk):
        self.nc, self.kb, self.stk = nc, kb, stk
        self.bufs = {}

    def buf(self, name):
        b = self.bufs.get(name)
        if b is None:
            b = Buf(name)
            self.bufs[name] = b
        return b

    def sbuf(self, name, shape, dtype):
        h = self.stk.enter_context(self.nc.sbuf_tensor(name, shape, dtype))
        setattr(self, name, h)
        self.buf(name)
        return h

    def psum(self, name, shape, dtype):
        h = self.stk.enter_context(self.nc.psum_tensor(name, shape, dtype))
        setattr(self, name, h)
        self.buf(name)
        return h

    def dram(self, name, shape, dtype, kind="ExternalInput"):
        if name in _DBG and kind == "Internal":
            kind = "ExternalOutput"
        t = self.nc.dram_tensor(name, shape, dtype, kind=kind)
        ap = t.ap()
        setattr(self, "d_" + name, ap)
        if kind != "ExternalInput":
            self.buf("d_" + name)
        return ap

    def declare(self):
        dr, sb, ps = self.dram, self.sbuf, self.psum
        dr("x0", [T, D], F32)
        dr("wkv", [D, DC], F32); dr("wkup", [DC, D], F32); dr("wvup", [DC, D], F32)
        dr("wqa", [D, D], F32); dr("wqr", [D, 48], F32); dr("wkr", [DC, 48], F32)
        dr("woa", [D, D], F32)
        for nm in ("g1", "b1", "g2", "b2"):
            dr(nm, [128, 8], F32)
        dr("wq", [M, D, NH * HD], F32); dr("wk", [M, D, NH * HD], F32)
        dr("wv", [M, D, NH * HD], F32); dr("wo", [M, NH * HD, D], F32)
        dr("bqc", [128, 16], F32); dr("bkc", [128, 16], F32)
        dr("bv", [M, NH * HD], F32); dr("bo", [M, D], F32)
        dr("mrw", [128, 8 * M], F32); dr("mrb", [1, M], F32)
        dr("frw", [128, 8 * NE], F32); dr("frb", [1, NE], F32)
        dr("fw1", [NE * D, FF], F32); dr("fw2", [NE * FF, D], F32)
        dr("ident", [128, 128], BF16)
        dr("cosT", [16, T], F32); dr("sinT", [16, T], F32)
        dr("dmask", [128, 896], BF16)
        dr("y", [T, D], F32, kind="ExternalOutput")
        for nm, shp, dt_ in (("dbg_PA", [128, 8 * T], BF16), ("dbg_PB", [128, 4 * T], BF16),
                             ("dbg_row", [128, 64], F32), ("dbg_PD", [128, 2 * T + NTT * 130], BF16),
                             ("dbg_qc", [128, 4096], BF16), ("dbg_WOH", [128, 4096], BF16),
                             ("dbg_ckv", [128, T], BF16)):
            if nm in _DBG:
                dr(nm, shp, dt_, kind="ExternalOutput")
        dr("sA", [T, D], F32, kind="Internal")
        dr("sB", [T, D], F32, kind="Internal")
        dr("xs1", [T, D], F32, kind="Internal")
        dr("xs2", [T, D], F32, kind="Internal")
        dr("ya", [T, D], F32, kind="Internal")
        dr("bcd", [8, 512], F32, kind="Internal")
        dr("aS", [T, D], F32, kind="Internal")

        sb("PA", [128, 8 * T], BF16)
        sb("PB", [128, 4 * T], BF16)
        sb("PD", [128, 2 * T + NTT * 130], BF16)
        for nm in ("PDq", "PDk", "PDv", "fw2v", "qcA"):
            self.buf(nm)
        sb("qcatA", [128, 4096], BF16)
        sb("PC", [128, 4096], BF16)
        sb("WOH", [128, 4096], BF16)
        sb("ckvT", [128, T], BF16)
        for i in range(3):
            sb(f"xt{i}", [128, D], F32)
        sb("ot0", [128, D], F32); sb("ot1", [128, D], F32)
        sb("zr0", [128, D], BF16); sb("zr1", [128, D], BF16)
        self.pbo = _RV(self.xt2, 0, 0, 1024)
        self.bufs["pbo"] = self.buf("xt2")
        self.krp1 = _TV(self.xt2, 0, 16, 0, 512)
        self.krp2 = _TV(self.xt2, 32, 16, 0, 512)
        self.rt1 = _TV(self.xt1, 64, 16, 0, 512)
        self.rt2 = _TV(self.ot0, 64, 16, 0, 512)
        self.cs0 = _TV(self.ot0, 0, 16, 0, 512)
        self.sn32 = _TV(self.ot0, 32, 16, 0, 512)
        self.sn0 = _TV(self.xt1, 0, 16, 0, 512)
        self.cs32 = _TV(self.xt1, 32, 16, 0, 512)
        self.rote = _TV(self.zr0, 64, 16, 0, 512)
        self.bufs["krp"] = self.buf("xt2")
        self.bufs["krp1"] = self.buf("xt2")
        self.bufs["krp2"] = self.buf("xt2")

        for nm in ("cs0", "sn32", "rt2"):
            self.bufs[nm] = self.buf("ot0")
        for nm in ("sn0", "cs32", "rt1"):
            self.bufs[nm] = self.buf("xt1")
        self.bufs["rote"] = self.buf("zr0")
        for i in range(3):
            sb(f"wT{i}", [128, 1024], BF16)
        self.craw0 = _TV(self.zr1, 0, 128, 0, 512)
        self.craw1 = _TV(self.zr1, 0, 128, 512, 512)
        self.bufs["craw0"] = self.buf("zr1"); self.bufs["craw1"] = self.buf("zr1")
        sb("rb0", [64, 512], F32); sb("rb1", [64, 512], F32)
        self.rz0 = _RV(self.ot1, 0, 0, 512)
        self.rz1 = _RV(self.ot1, 32, 0, 512)
        self.bufs["rz0"] = self.buf("ot1"); self.bufs["rz1"] = self.buf("ot1")
        sb("stats", [128, 64], F32); sb("st6a", [128, 12], F32); sb("st6b", [128, 12], F32)
        sb("scomb", [128, 32], F32)
        sb("lnw0", [128, 32], F32); sb("lnw1", [128, 32], F32)
        sb("lnw2", [128, 32], F32); sb("lnw3", [128, 32], F32)

        sb("io8r_u", [1, 8], U32); sb("io8r", [1, 8], F32)
        sb("sident", [128, 128], BF16)
        sb("ones_col", [128, 1], BF16); sb("onesR", [128, 128], BF16)
        sb("g1c", [128, 8], F32); sb("b1c", [128, 8], F32)
        sb("g2c", [128, 8], F32); sb("b2c", [128, 8], F32)
        sb("b1h", [128, 8], BF16); sb("b1l", [128, 8], BF16)
        sb("b2h", [128, 8], BF16); sb("b2l", [128, 8], BF16)
        sb("bqcs", [128, 16], F32); sb("bkcs", [128, 16], F32)
        sb("ROWA", [128, 2048], F32)
        sb("ROWC", [128, 2048], BF16)
        sb("VBR", [128, 384], BF16)
        sb("VBF", [128, 256], F32)
        self.bvr = _RV(self.ROWA, 0, 0, 2048)
        self.bor = _RV(self.ROWA, 32, 0, 2048)
        self.pboh = _RV(self.ROWC, 0, 0, 1024)
        self.pbol = _RV(self.ROWC, 0, 1024, 1024)
        for nm in ("bvr", "bor", "pboh", "pbol"):
            self.buf(nm)
        sb("mrwb", [128, 8 * M], BF16); sb("frwb", [128, 8 * NE], BF16)
        sb("mrbr", [1, M], F32); sb("frbr", [1, NE], F32)
        sb("dmk", [128, 896], BF16)
        self.csum = _RV(self.ROWA, 64, 0, 1024)
        self.csbf = _RV(self.ROWC, 64, 0, 1024)
        self.buf("csum"); self.buf("csbf")
        sb("csT", [128, 8], F32); sb("mnc", [128, 8], F32); sb("mncb", [128, 8], BF16)
        sb("rrow", [1, NE], F32)
        sb("probs", [1, M], F32)
        sb("t0", [1, 8], F32); sb("t1", [1, 8], F32); sb("t2", [1, 8], F32)
        sb("t3", [1, 1], F32); sb("t4", [1, 1], F32); sb("t5", [1, 1], F32)
        sb("t6", [1, 1], F32); sb("t7", [1, 1], F32)
        sb("eidx", [1, 2], F32)
        sb("epv", [1, 2], F32)
        sb("biasq", [128, 1], F32); sb("biask", [128, 1], F32)
        sb("hbias", [128, 16], F32)
        sb("io8f", [128, 8], F32); sb("io16f", [128, 16], F32)
        sb("io8u", [128, 8], U32); sb("io16u", [128, 16], U32)
        sb("ii1", [128, 8], U32); sb("ii2", [128, 16], U32)
        sb("iif1", [128, 8], F32); sb("iif2", [128, 16], F32)
        for i in range(4):
            sb(f"bcc{i}", [128, 1], F32)
        sb("p1c", [128, 1], F32); sb("p2c", [128, 1], F32)

        ps("psA", [128, 1024], F32)
        ps("psB", [128, 1024], F32)
        for nm in ("psA_0", "psA_1", "psB_0", "psB_1"):
            self.buf(nm)
        ps("pc0", [128, 512], F32)
        ps("pc1", [128, 512], F32)
        ps("pg", [128, 512], F32)
        ps("ptr", [128, 1024], BF16)

    def mm(self, out_ap, lhsT_ap, rhs_ap, reads, writes, start=True, stop=True):
        return self.kb.op("pe", lambda h, o=out_ap, l=lhsT_ap, r=rhs_ap, st=start, sp=stop:
                          h.matmul(o, l, r, start=st, stop=sp), reads=reads, writes=writes)

    def mm_group(self, fns, reads, writes):
        return self.kb.op_group("pe", fns, reads=reads, writes=writes)

    def act(self, out_ap, in_ap, func, reads, writes, bias=None, scale=None, accum=None):
        kw = {}
        if bias is not None:
            kw["bias"] = bias
        if scale is not None:
            kw["scale"] = scale
        if accum is not None:
            kw["accum_out"] = accum
        return self.kb.op("act", lambda h, o=out_ap, i=in_ap, f=func, k=dict(kw):
                          h.activation(o, i, f, **k), reads=reads, writes=writes)

    def dve(self, meth, *args, reads=(), writes=(), **kw):
        return self.kb.op("dve", lambda h, m=meth, a=args, k=dict(kw):
                          getattr(h, m)(*a, **k), reads=reads, writes=writes)

    def pool(self, meth, *args, reads=(), writes=(), **kw):
        return self.kb.op("pool", lambda h, m=meth, a=args, k=dict(kw):
                          getattr(h, m)(*a, **k), reads=reads, writes=writes)

    def bcast_row(self, src_row_ap, src_reads, dst_ap, dst_buf, slot, npart, width):
        kb = self.kb
        slot_buf = self.buf(f"bcd{slot}")
        dst_dram = bass.AP(self.d_bcd.tensor, slot * 512, [[1, 1], [1, width]])
        kb.dma(dst_dram, src_row_ap, reads=src_reads, writes=[slot_buf])
        bc = bass.AP(self.d_bcd.tensor, slot * 512, [[0, npart], [1, width]])
        return kb.dma(dst_ap, bc, reads=[slot_buf], writes=[dst_buf])

    def sap(self, handle, offset, free_dims, npart=128):
        base = handle[:]
        pitch = base.ap[0][0]
        return bass.AP(base.tensor, offset, [[pitch, npart]] + [list(d) for d in free_dims])

    def ps_mode(self, mode):
        if mode == getattr(self, "_psmode", None):
            return
        self._psmode = mode
        kb = self.kb
        if mode == "half":
            for nm in ("A", "B"):
                f = self.buf("ps" + nm)
                kb.alias(self.buf(f"ps{nm}_0"), [f])
                kb.alias(self.buf(f"ps{nm}_1"), [f])
        else:
            for nm in ("A", "B"):
                kb.alias(self.buf("ps" + nm),
                         [self.buf(f"ps{nm}_0"), self.buf(f"ps{nm}_1")])

    def ps_half(self, i, w=512):
        t = (self.psA, self.psA, self.psB, self.psB)[i % 4]
        lo = (0, 512, 0, 512)[i % 4]
        nm = ("psA_0", "psA_1", "psB_0", "psB_1")[i % 4]
        return t[:, lo:lo + w], self.buf(nm)

    def emit_setup(self):
        kb, b = self.kb, self.buf
        kb.dma(self.sident[:], self.d_ident, writes=[b("sident")])
        kb.dma(self.dmk[:], self.d_dmask, writes=[b("dmk")])
        for nm in ("g1", "b1", "g2", "b2"):
            kb.dma(getattr(self, nm + "c")[:], getattr(self, "d_" + nm),
                   writes=[b(nm + "c")])
        kb.dma(self.bqcs[:], self.d_bqc, writes=[b("bqcs")])
        kb.dma(self.bkcs[:], self.d_bkc, writes=[b("bkcs")])
        for m in range(M):
            kb.dma(self.bvr[0:1, m * 1024:(m + 1) * 1024], self.d_bv[m:m + 1, :],
                   writes=[b("bvr")])
            kb.dma(self.bor[0:1, m * 1024:(m + 1) * 1024], self.d_bo[m:m + 1, :],
                   writes=[b("bor")])
        kb.dma(self.stats[:, 0:16], self.d_mrw, writes=[b("stats")])
        self.dve("tensor_copy", self.mrwb[:], self.stats[:, 0:16],
                 reads=[b("stats")], writes=[b("mrwb")])
        kb.dma(self.stats[:, 0:64], self.d_frw, writes=[b("stats")])
        self.dve("tensor_copy", self.frwb[:], self.stats[:, 0:64],
                 reads=[b("stats")], writes=[b("frwb")])
        kb.dma(self.mrbr[:], self.d_mrb, writes=[b("mrbr")])
        kb.dma(self.frbr[:], self.d_frb, writes=[b("frbr")])
        for nm in ("b1", "b2"):
            c, h_, l_ = b(nm + "c"), b(nm + "h"), b(nm + "l")
            hi, lo = getattr(self, nm + "h"), getattr(self, nm + "l")
            self.dve("tensor_copy", hi[:], getattr(self, nm + "c")[:],
                     reads=[c], writes=[h_])
            self.dve("tensor_tensor", lo[:], getattr(self, nm + "c")[:], hi[:],
                     ALU.subtract, reads=[c, h_], writes=[l_])
        for t in (self.rrow, self.eidx, self.epv, self.probs, self.stats, self.csT, self.mnc):
            self.dve("memset", t[:], 0.0, writes=[self.buf("stats")])
        self.dve("memset", self.ones_col[:], 1.0, writes=[b("ones_col")])
        self.dve("memset", self.onesR[:], 1.0, writes=[b("onesR")])
        vp_ap = self.sap(self.PD, 2 * T + 64, [[130, NTT], [65, 2]])
        self.dve("memset", vp_ap, 1.0, writes=[b("PDv")])
        self.pool("iota", self.io8u[:], [[128, 8]], base=0, channel_multiplier=1,
                  writes=[b("io8u")])
        self.pool("iota", self.io16u[:], [[128, 16]], base=0, channel_multiplier=1,
                  writes=[b("io16u")])
        self.pool("iota", self.io8r_u[:], [[1, 8]], base=0, channel_multiplier=0,
                  writes=[b("io8r_u")])
        self.dve("tensor_copy", self.io8f[:], self.io8u[:], reads=[b("io8u")], writes=[b("io8f")])
        self.dve("tensor_copy", self.io16f[:], self.io16u[:], reads=[b("io16u")], writes=[b("io16f")])
        self.dve("tensor_copy", self.io8r[:], self.io8r_u[:], reads=[b("io8r_u")], writes=[b("io8r")])

    def emit_ln(self, src_ap, src_buf, gcol, fused, colsum, stats_ready=False):
        kb, b = self.kb, self.buf
        xts = [self.xt0, self.xt1, self.xt2]
        xbs = [b("xt0"), b("xt1"), b("xt2")]
        if not stats_ready:
            for tt in range(NTT):
                xt, xb = xts[tt % 3], xbs[tt % 3]
                rd = [src_buf] if src_buf else []
                kb.dma(xt[:], src_ap[tt * 128:(tt + 1) * 128, :], reads=rd, writes=[xb])
                st6 = (self.st6a, self.st6b)[tt % 2]
                s6b = (b("st6a"), b("st6b"))[tt % 2]
                self.dve("bn_stats", st6[:, 0:6], xt[:, 0:512], reads=[xb], writes=[s6b])
                self.dve("bn_stats", st6[:, 6:12], xt[:, 512:1024], reads=[xb], writes=[s6b])
                self.dve("bn_aggr", self.stats[:, 2 * tt:2 * tt + 2], st6[:, 0:12],
                         reads=[s6b], writes=[b("stats")])
        varv = self.sap(self.stats, 1, [[2, 32]])
        sb_, ssb = b("stats"), b("scomb")
        w0, w1, w2, w3 = b("lnw0"), b("lnw1"), b("lnw2"), b("lnw3")
        e1 = EPS_OUT if fused else (EPS_OUT if gcol is None else EPS_IN)
        self.dve("tensor_scalar", self.lnw0[:], varv, e1, None, ALU.add,
                 reads=[sb_], writes=[w0])
        self.act(self.lnw1[:], self.lnw0[:], AF.Ln, reads=[w0], writes=[w1])
        if fused:
            self.dve("reciprocal", self.lnw2[:], self.lnw0[:], reads=[w0], writes=[w2])
            self.dve("tensor_tensor", self.lnw3[:], varv, self.lnw2[:], ALU.mult,
                     reads=[sb_, w2], writes=[w3])
            self.dve("tensor_scalar", self.lnw3[:], self.lnw3[:], EPS_IN, None, ALU.add,
                     reads=[w3], writes=[w3])
            self.act(self.lnw2[:], self.lnw1[:], AF.Exp, scale=-0.5, reads=[w1], writes=[w2])
            self.act(self.lnw0[:], self.lnw3[:], AF.Ln, reads=[w3], writes=[w0])
            self.dve("tensor_tensor", self.lnw1[:], self.lnw1[:], self.lnw0[:], ALU.add,
                     reads=[w1, w0], writes=[w1])
        self.act(self.scomb[:], self.lnw1[:], AF.Exp, scale=-0.5, reads=[w1], writes=[ssb])
        PAb = b("PA")
        for tt in range(NTT):
            xt, xb = xts[tt % 3], xbs[tt % 3]
            rd = [src_buf] if src_buf else []
            kb.dma(xt[:], src_ap[tt * 128:(tt + 1) * 128, :], reads=rd, writes=[xb])
            zr = (self.zr0, self.zr1)[tt % 2]
            zb = (b("zr0"), b("zr1"))[tt % 2]
            mcol = self.sap(self.stats, 2 * tt, [[1, 1]])
            scol = self.scomb[:, tt:tt + 1]
            self.dve("tensor_scalar", zr[:], xt[:], mcol, scol, ALU.subtract, ALU.mult,
                     reads=[xb, sb_, ssb], writes=[zb])
            if fused:
                ott = (self.ot0, self.ot1)[tt % 2]
                otb = (b("ot0"), b("ot1"))[tt % 2]
                self.dve("tensor_scalar", ott[:], xt[:], mcol,
                         self.lnw2[:, tt:tt + 1], ALU.subtract, ALU.mult,
                         reads=[xb, sb_, w2], writes=[otb])
                kb.dma(self.d_aS[tt * 128:(tt + 1) * 128, :], ott[:],
                       reads=[otb], writes=[b("d_aS")], q="sc")
            if colsum:
                self.mm(self.pg[0:1, 0:512], self.ones_col[:], zr[:, 0:512],
                        reads=[b("ones_col"), zb],
                        writes=[b("pg")] if tt in (0, NTT - 1) else (),
                        start=(tt == 0), stop=(tt == NTT - 1))
                self.mm(self.pc0[0:1, 0:512], self.ones_col[:], zr[:, 512:1024],
                        reads=[b("ones_col"), zb],
                        writes=[b("pc0")] if tt in (0, NTT - 1) else (),
                        start=(tt == 0), stop=(tt == NTT - 1))
            fns = []
            for dt in range(8):
                fns.append(lambda h, o=self.ptr[:, dt * 128:(dt + 1) * 128],
                           i=zr[:, dt * 128:(dt + 1) * 128], e=self.sident[:]:
                           h.transpose(o, i, e))
            self.mm_group(fns, reads=[zb, b("sident")], writes=[b("ptr")])
            for dt in range(8):
                dst = self.PA[:, dt * T + tt * 128: dt * T + (tt + 1) * 128]
                srcp = self.ptr[:, dt * 128:(dt + 1) * 128]
                if gcol is not None:
                    self.dve("tensor_scalar", dst, srcp, gcol[:, dt:dt + 1], None,
                             ALU.mult, reads=[b("ptr"), b(gcol is self.g1c and "g1c" or "g2c")],
                             writes=[PAb])
                else:
                    self.dve("tensor_copy", dst, srcp, reads=[b("ptr")], writes=[PAb])
        if colsum:
            self.dve("tensor_copy", self.csum[0:1, 0:512], self.pg[0:1, 0:512],
                     reads=[b("pg")], writes=[b("csum")])
            self.dve("tensor_copy", self.csum[0:1, 512:1024], self.pc0[0:1, 0:512],
                     reads=[b("pc0")], writes=[b("csum")])
            self.dve("tensor_copy", self.csbf[0:1, 0:1024], self.csum[0:1, 0:1024], reads=[b("csum")], writes=[b("csbf")])
            fns = []
            for dt in range(8):
                fns.append(lambda h, o=self.ptr[:, 2 * dt:2 * dt + 1],
                           i=self.csbf[0:1, dt * 128:(dt + 1) * 128],
                           e=self.sident[64:65, 64:65]:
                           h.transpose(o, i, e))
            self.mm_group(fns, reads=[b("csbf"), b("sident")], writes=[b("ptr")])
            self.dve("tensor_copy", self.csT[:], self.sap(self.ptr, 0, [[2, 8]]),
                     reads=[b("ptr")], writes=[b("csT")])

    def emit_meancol(self, gcol, bcol, gname, bname):
        b = self.buf
        self.dve("tensor_scalar", self.mnc[:], self.csT[:], 1.0 / T, None, ALU.mult,
                 reads=[b("csT")], writes=[b("mnc")])
        self.dve("tensor_tensor", self.mnc[:], self.mnc[:], gcol[:], ALU.mult,
                 reads=[b("mnc"), b(gname)], writes=[b("mnc")])
        self.dve("tensor_tensor", self.mnc[:], self.mnc[:], bcol[:], ALU.add,
                 reads=[b("mnc"), b(bname)], writes=[b("mnc")])
        self.dve("tensor_copy", self.mncb[:], self.mnc[:], reads=[b("mnc")], writes=[b("mncb")])

    def emit_router(self, nr, rwb, rwb_name, rb_row, rb_name):
        b = self.buf
        fns = []
        for dt in range(8):
            fns.append(lambda h, o=self.pg[0:1, 0:nr], l=self.mncb[:, dt:dt + 1],
                       r=rwb[:, dt * nr:(dt + 1) * nr], st=(dt == 0), sp=(dt == 7):
                       h.matmul(o, l, r, start=st, stop=sp))
        self.mm_group(fns, reads=[b("mncb"), b(rwb_name)], writes=[b("pg")])
        self.dve("tensor_tensor", self.rrow[0:1, 0:nr], self.pg[0:1, 0:nr], rb_row,
                 ALU.add, reads=[b("pg"), b(rb_name)], writes=[b("rrow")])

    def emit_probs2(self):
        b = self.buf
        self.act(self.t0[0:1, 0:2], self.rrow[0:1, 0:2], AF.Exp,
                 reads=[b("rrow")], writes=[b("t0")])
        self.dve("reduce_sum", self.t3[:], self.t0[0:1, 0:2], axis=AX.X,
                 reads=[b("t0")], writes=[b("t3")])
        self.dve("reciprocal", self.t4[:], self.t3[:], reads=[b("t3")], writes=[b("t4")])
        self.dve("tensor_scalar", self.probs[:], self.t0[0:1, 0:2], self.t4[:], None,
                 ALU.mult, reads=[b("t0"), b("t4")], writes=[b("probs")])

    def emit_topk(self):
        b = self.buf
        r = self.rrow[0:1, 0:NE]
        self.dve("reduce_max", self.t3[:], r, axis=AX.X, reads=[b("rrow")], writes=[b("t3")])
        self.dve("tensor_scalar", self.t0[0:1, 0:NE], r, self.t3[:], None, ALU.is_ge,
                 reads=[b("rrow"), b("t3")], writes=[b("t0")])
        self.dve("tensor_scalar", self.t1[0:1, 0:NE], self.t0[0:1, 0:NE], 1e30, None,
                 ALU.mult, reads=[b("t0")], writes=[b("t1")])
        self.dve("tensor_tensor", self.t2[0:1, 0:NE], r, self.t1[0:1, 0:NE],
                 ALU.subtract, reads=[b("rrow"), b("t1")], writes=[b("t2")])
        self.dve("reduce_max", self.t4[:], self.t2[0:1, 0:NE], axis=AX.X,
                 reads=[b("t2")], writes=[b("t4")])
        self.dve("tensor_scalar", self.t1[0:1, 0:NE], self.t2[0:1, 0:NE], self.t4[:],
                 None, ALU.is_ge, reads=[b("t2"), b("t4")], writes=[b("t1")])
        self.dve("tensor_tensor", self.t2[0:1, 0:NE], self.t0[0:1, 0:NE], self.io8r[:],
                 ALU.mult, reads=[b("t0"), b("io8r")], writes=[b("t2")])
        self.dve("reduce_sum", self.eidx[0:1, 0:1], self.t2[0:1, 0:NE], axis=AX.X,
                 reads=[b("t2")], writes=[b("eidx")])
        self.dve("tensor_tensor", self.t2[0:1, 0:NE], self.t1[0:1, 0:NE], self.io8r[:],
                 ALU.mult, reads=[b("t1"), b("io8r")], writes=[b("t2")])
        self.dve("reduce_sum", self.eidx[0:1, 1:2], self.t2[0:1, 0:NE], axis=AX.X,
                 reads=[b("t2")], writes=[b("eidx")])
        self.dve("tensor_tensor", self.t5[:], self.t4[:], self.t3[:], ALU.subtract,
                 reads=[b("t4"), b("t3")], writes=[b("t5")])
        self.act(self.t6[:], self.t5[:], AF.Exp, reads=[b("t5")], writes=[b("t6")])
        self.dve("tensor_scalar", self.t7[:], self.t6[:], 1.0, None, ALU.add,
                 reads=[b("t6")], writes=[b("t7")])
        self.dve("reciprocal", self.t3[:], self.t7[:], reads=[b("t7")], writes=[b("t3")])
        self.dve("tensor_copy", self.epv[0:1, 0:1], self.t3[:], reads=[b("t3")], writes=[b("epv")])
        self.dve("tensor_tensor", self.epv[0:1, 1:2], self.t6[:], self.t3[:], ALU.mult,
                 reads=[b("t6"), b("t3")], writes=[b("epv")])

    def emit_attention(self, hp, ksrc, qsrc, scale, s_count_fn, mask_last4,
                       pm_ap, pm_b, ctx_off, gidx0):
        kb, b = self.kb, self.buf
        self.ps_mode("full")
        gi = gidx0
        (kh0, kl0, kh0hi, kc0, kb0), (kh1, kl1, kh1hi, kc1, kb1) = ksrc
        (qh0, ql0, qh0hi, qc0, qb0), (qh1, ql1, qh1hi, qc1, qb1) = qsrc
        for qch in range(NCH):
            ns = s_count_fn(qch)

            def emit_ctx(si, wT, wb, ns=ns):
                vbase = 2 * T + si * 130
                for h2 in range(2):
                    pc = (self.pc0, self.pc1)[h2]
                    pcb = (b("pc0"), b("pc1"))[h2]
                    vsl = self.PD[0:128, vbase + h2 * 65: vbase + h2 * 65 + 65]
                    wr = [pcb] if (si == 0 or si == ns - 1) else ()
                    self.mm(pc[0:65, 0:512], vsl, wT[:, h2 * 512:(h2 + 1) * 512],
                            reads=[b("PDv"), wb], writes=wr,
                            start=(si == 0), stop=(si == ns - 1))

            pend = []
            for si in range(ns):
                ps = (self.psA, self.psB)[gi % 2]
                psb = (b("psA"), b("psB"))[gi % 2]
                st = si * 128
                l0 = kh0[kl0:kh0hi, kc0 + st: kc0 + st + 128]
                r0 = qh0[ql0:qh0hi, qc0 + qch * 512: qc0 + (qch + 1) * 512]
                l1 = kh1[kl1:kh1hi, kc1 + st: kc1 + st + 128]
                r1 = qh1[ql1:qh1hi, qc1 + qch * 512: qc1 + (qch + 1) * 512]
                fns = [
                    lambda h, o=ps[:, 0:512], l=l0, r=r0: h.matmul(o, l, r, start=True, stop=True),
                    lambda h, o=ps[:, 512:1024], l=l1, r=r1: h.matmul(o, l, r, start=True, stop=True),
                ]
                self.mm_group(fns, reads=[kb0, kb1, qb0, qb1], writes=[psb])
                wT = (self.wT0, self.wT1, self.wT2)[gi % 3]
                wb = (b("wT0"), b("wT1"), b("wT2"))[gi % 3]
                self.act(wT[:], ps[:], AF.Exp, scale=scale, reads=[psb], writes=[wb])
                if mask_last4 and si >= ns - 4:
                    k = si - (ns - 4)
                    mk = self.dmk[:, (3 - k) * 128:(3 - k) * 128 + 512]
                    self.dve("tensor_tensor", wT[:, 0:512], wT[:, 0:512], mk,
                             ALU.mult, reads=[wb, b("dmk")], writes=[wb])
                    self.dve("tensor_tensor", wT[:, 512:1024], wT[:, 512:1024], mk,
                             ALU.mult, reads=[wb, b("dmk")], writes=[wb])
                pend.append((si, wT, wb))
                if len(pend) > 2:
                    emit_ctx(*pend.pop(0))
                gi += 1
            for pp in pend:
                emit_ctx(*pp)
            hold = []
            for h2 in range(2):
                pc = (self.pc0, self.pc1)[h2]
                pcb = (b("pc0"), b("pc1"))[h2]
                rz = (self.rz0, self.rz1)[h2]
                rzb = (b("rz0"), b("rz1"))[h2]
                slot = (qch * 2 + h2) % 2
                rbt = (self.rb0, self.rb1)[slot]
                rbb = (b("rb0"), b("rb1"))[slot]
                craw = (self.craw0, self.craw1)[h2]
                crb = (b("craw0"), b("craw1"))[h2]
                self.dve("reciprocal", rz[0:1, 0:512], pc[64:65, 0:512], reads=[pcb], writes=[rzb])
                if pm_ap is not None:
                    self.dve("tensor_scalar", rz[0:1, 0:512], rz[0:1, 0:512], pm_ap,
                             None, ALU.mult, reads=[rzb, pm_b], writes=[rzb])
                self.dve("tensor_copy", craw[0:64, :], pc[0:64, 0:512],
                         reads=[pcb], writes=[crb])
                dst = self.PB[h2 * 64:(h2 + 1) * 64,
                              ctx_off + qch * 512: ctx_off + (qch + 1) * 512]
                if _NOBC:
                    self.dve("tensor_copy", dst, craw[0:64, :],
                             reads=[crb], writes=[b("PB")])
                else:
                    self.bcast_row(rz[0:1, 0:512], [rzb], rbt[:], rbb, slot, 64, 512)
                    hold.append((dst, craw, crb, rbt, rbb))
            for (dst, craw, crb, rbt, rbb) in hold:
                self.dve("tensor_tensor", dst, craw[0:64, :], rbt[:], ALU.mult,
                         reads=[crb, rbb], writes=[b("PB")])
        return gi

    def emit_gen_ffn(self, m, hp):
        kb, b = self.kb, self.buf
        wqv = self.PC[:, 0:1024].rearrange("p (dt c) -> p dt c", dt=8)
        wkvv = self.PC[:, 1024:2048].rearrange("p (dt c) -> p dt c", dt=8)
        src_q = self.d_wq[m, :, hp * 128:(hp + 1) * 128].rearrange("(dt p) c -> p dt c", p=128)
        src_k = self.d_wk[m, :, hp * 128:(hp + 1) * 128].rearrange("(dt p) c -> p dt c", p=128)
        kb.pool_dma(wqv, src_q, writes=[b("PCq")])
        kb.pool_dma(wkvv, src_k, writes=[b("PCk")])
        wvv = self.PC[:, 2048:3072].rearrange("p (dt c) -> p dt c", dt=8)
        src_v = self.d_wv[m, :, hp * 128:(hp + 1) * 128].rearrange("(dt p) c -> p dt c", p=128)
        kb.pool_dma(wvv, src_v, writes=[b("PCv")])
        if _CUT == f"g{hp}load" and m == 0:
            for (ofs, bn) in ((0, "PCq"), (1024, "PCk"), (2048, "PCv")):
                self.dve("tensor_copy", self.VBF[0:1, 0:128],
                         self.PC[0:1, ofs:ofs + 128], reads=[b(bn)], writes=[b("VBF")])
            return True
        for (wofs, wbuf, bio, bname, bc, bcn) in (
                (0, b("PCq"), self.biasq, "biasq", self.bqcs, "bqcs"),
                (1024, b("PCk"), self.biask, "biask", self.bkcs, "bkcs")):
            fns = []
            n = 0
            for dt in range(8):
                for (hl, hlb) in ((self.b1h, "b1h"), (self.b1l, "b1l")):
                    fns.append(lambda h, o=self.pg[0:128, 0:1],
                               l=self.PC[:, wofs + dt * 128: wofs + (dt + 1) * 128],
                               r=hl[:, dt:dt + 1], st=(n == 0), sp=(n == 15):
                               h.matmul(o, l, r, start=st, stop=sp))
                    n += 1
            self.mm_group(fns, reads=[wbuf, b("b1h"), b("b1l")], writes=[b("pg")])
            self.dve("tensor_scalar", bio[:], self.pg[0:128, 0:1],
                     bc[:, m * 8 + hp: m * 8 + hp + 1], None, ALU.add,
                     reads=[b("pg"), b(bcn)], writes=[b(bname)])
        fns = []
        n = 0
        for dt in range(8):
            for hl in (self.b1h, self.b1l):
                fns.append(lambda h, o=self.pg[0:1, 0:128], l=hl[:, dt:dt + 1],
                           r=self.PC[:, 2048 + dt * 128: 2048 + (dt + 1) * 128],
                           st=(n == 0), sp=(n == 15):
                           h.matmul(o, l, r, start=st, stop=sp))
                n += 1
        self.mm_group(fns, reads=[b("PCv"), b("b1h"), b("b1l")], writes=[b("pg")])
        self.dve("tensor_tensor", self.VBF[0:1, 0:128], self.pg[0:1, 0:128],
                 self.bvr[0:1, m * 1024 + hp * 128: m * 1024 + (hp + 1) * 128],
                 ALU.add, reads=[b("pg"), b("bvr")], writes=[b("VBF")])
        self.dve("tensor_copy", self.VBR[0:1, 0:128], self.VBF[0:1, 0:128],
                 reads=[b("VBF")], writes=[b("VBR")])
        self.dve("tensor_tensor", self.VBF[0:1, 128:256], self.VBF[0:1, 0:128],
                 self.VBR[0:1, 0:128], ALU.subtract, reads=[b("VBF"), b("VBR")],
                 writes=[b("VBF")])
        self.dve("tensor_copy", self.VBR[0:1, 128:256], self.VBF[0:1, 128:256],
                 reads=[b("VBF")], writes=[b("VBR")])
        if _CUT == f"g{hp}bias" and m == 0:
            return True
        self.ps_mode("half")
        ri = 0
        for (wofs, wbuf, dbase, dstb, bio, bname) in (
                (0, b("PCq"), 0, b("PDq"), self.biasq, "biasq"),
                (1024, b("PCk"), T, b("PDk"), self.biask, "biask")):
            for tch in range(8):
                pt, ptb = self.ps_half(ri)
                ri += 1
                fns = []
                for dt in range(8):
                    fns.append(lambda h, o=pt,
                               l=self.PC[:, wofs + dt * 128: wofs + (dt + 1) * 128],
                               r=self.PA[:, dt * T + tch * 512: dt * T + (tch + 1) * 512],
                               st=(dt == 0), sp=(dt == 7):
                               h.matmul(o, l, r, start=st, stop=sp))
                self.mm_group(fns, reads=[wbuf, b("PA")], writes=[ptb])
                self.dve("tensor_scalar",
                         self.PD[:, dbase + tch * 512: dbase + (tch + 1) * 512],
                         pt, bio[:], None, ALU.add,
                         reads=[ptb, b(bname)], writes=[dstb])
        if _CUT == f"g{hp}qk" and m == 0:
            self.ps_mode("full")
            return True
        for tt in range(NTT):
            pt, ptb = self.ps_half(ri, w=128)
            ri += 1
            fns = []
            for dt in range(8):
                fns.append(lambda h, o=pt,
                           l=self.PA[:, dt * T + tt * 128: dt * T + (tt + 1) * 128],
                           r=self.PC[:, 2048 + dt * 128: 2048 + (dt + 1) * 128],
                           st=(dt == 0), sp=False:
                           h.matmul(o, l, r, start=st, stop=sp))
            if _NOVB:
                fns[-1] = (lambda h, o=pt,
                           l=self.PA[:, 7 * T + tt * 128: 7 * T + (tt + 1) * 128],
                           r=self.PC[:, 2048 + 7 * 128: 2048 + 8 * 128]:
                           h.matmul(o, l, r, start=False, stop=True))
            else:
                for (c0, last) in ((0, False), (128, True)):
                    fns.append(lambda h, o=pt,
                               l=self.onesR[0:1, :],
                               r=self.VBR[0:1, c0:c0 + 128], sp=last:
                               h.matmul(o, l, r, start=False, stop=sp))
            self.mm_group(fns, reads=[b("PA"), b("PCv"), b("onesR"),
                                      b("VBR")], writes=[ptb])
            dstv = self.sap(self.PD, 2 * T + tt * 130, [[65, 2], [1, 64]])
            self.dve("tensor_copy", dstv,
                     pt.rearrange("p (g c) -> p g c", c=64),
                     reads=[ptb], writes=[b("PDv")])
        self.ps_mode("full")

    def emit_proj_pass(self, wo_src_ap, half, src_ap, src_buf, dst_ap, dst_buf,
                      bias_rows, with_stats):
        kb, b = self.kb, self.buf
        wov = self.WOH[:].rearrange("p (hh d) -> p hh d", hh=4)
        kb.pool_dma(wov, wo_src_ap, writes=[b("WOH")])
        if _CUT == "mha_w0":
            return
        self.ps_mode("half")
        for tt in range(NTT):
            xt, xb = (self.xt0, self.xt1)[tt % 2], (b("xt0"), b("xt1"))[tt % 2]
            rd = [src_buf] if src_buf else []
            kb.dma(xt[:], src_ap[tt * 128:(tt + 1) * 128, :], reads=rd, writes=[xb])
            ot, ob = (self.ot0, self.ot1)[tt % 2], (b("ot0"), b("ot1"))[tt % 2]
            for dch in range(2):
                pst, psb = self.ps_half(tt * 2 + dch)
                fns = []
                for hh in range(4):
                    fns.append(lambda h, o=pst,
                               l=self.PB[:, hh * T + tt * 128: hh * T + (tt + 1) * 128],
                               r=self.WOH[:, hh * 1024 + dch * 512: hh * 1024 + (dch + 1) * 512],
                               st=(hh == 0), sp=(hh == 3 and bias_rows is None):
                               h.matmul(o, l, r, start=st, stop=sp))
                rds = [b("PB"), b("WOH")]
                if bias_rows is not None:
                    for j, (br, brn, obase) in enumerate(bias_rows):
                        fns.append(lambda h, o=pst,
                                   l=self.onesR[obase:obase + 1, :],
                                   r=br[0:1, dch * 512:(dch + 1) * 512], sp=(j == 1):
                                   h.matmul(o, l, r, start=False, stop=sp))
                        rds.append(b(brn))
                    rds.append(b("onesR"))
                self.mm_group(fns, reads=rds, writes=[psb])
                self.dve("tensor_tensor", ot[:, dch * 512:(dch + 1) * 512], pst,
                         xt[:, dch * 512:(dch + 1) * 512], ALU.add,
                         reads=[psb, xb], writes=[ob])
            if with_stats:
                st6 = (self.st6a, self.st6b)[tt % 2]
                s6b = (b("st6a"), b("st6b"))[tt % 2]
                self.dve("bn_stats", st6[:, 0:6], ot[:, 0:512], reads=[ob], writes=[s6b])
                self.dve("bn_stats", st6[:, 6:12], ot[:, 512:1024], reads=[ob], writes=[s6b])
                self.dve("bn_aggr", self.stats[:, 2 * tt:2 * tt + 2], st6[:, 0:12],
                         reads=[s6b], writes=[b("stats")])
            kb.dma(dst_ap[tt * 128:(tt + 1) * 128, :], ot[:], reads=[ob],
                   writes=[dst_buf], q="sc")
        self.ps_mode("full")

    def emit_expert_select_gather(self, slot):
        kb, b = self.kb, self.buf
        bcc = (self.bcc0, self.bcc1)[slot]
        bccb = (b("bcc0"), b("bcc1"))[slot]
        bcc2 = (self.bcc2, self.bcc3)[slot]
        bcc2b = (b("bcc2"), b("bcc3"))[slot]
        self.dve("tensor_scalar", self.t5[:], self.eidx[0:1, slot:slot + 1], 1024.0,
                 None, ALU.mult, reads=[b("eidx")], writes=[b("t5")])
        self.bcast_row(self.t5[:], [b("t5")], bcc[:], bccb, 4 + slot, 128, 1)
        self.dve("tensor_scalar", self.iif1[:], self.io8f[:], bcc[:], None, ALU.add,
                 reads=[b("io8f"), bccb], writes=[b("iif1")])
        self.dve("tensor_copy", self.ii1[:], self.iif1[:], reads=[b("iif1")], writes=[b("ii1")])
        self.dve("tensor_scalar", self.t5[:], self.eidx[0:1, slot:slot + 1], 2048.0,
                 None, ALU.mult, reads=[b("eidx")], writes=[b("t5")])
        self.bcast_row(self.t5[:], [b("t5")], bcc2[:], bcc2b, 6 + slot, 128, 1)
        self.dve("tensor_scalar", self.iif2[:], self.io16f[:], bcc2[:], None, ALU.add,
                 reads=[b("io16f"), bcc2b], writes=[b("iif2")])
        self.dve("tensor_copy", self.ii2[:], self.iif2[:], reads=[b("iif2")], writes=[b("ii2")])
        for dt in range(8):
            kb.pool_dma(self.PB[:, dt * FF:(dt + 1) * FF], self.d_fw1,
                        reads=[b("ii1")], writes=[b("PB")],
                        indirect=bass.IndirectOffsetOnAxis(self.ii1[:, dt:dt + 1], 0))
        for ft in range(16):
            if ft < 12:
                dst = self.PD[:, ft * 1024:(ft + 1) * 1024]
                dbf = b("fw2v")
            else:
                dst = self.qcatA[:, (ft - 12) * 1024:(ft - 11) * 1024]
                dbf = b("qcA")
            kb.pool_dma(dst, self.d_fw2, reads=[b("ii2")], writes=[dbf],
                        indirect=bass.IndirectOffsetOnAxis(self.ii2[:, ft:ft + 1], 0))

    def emit_expert_compute(self, slot, ci, xsrc_ap, xsrc_buf, dst_ap, dst_buf):
        kb, b = self.kb, self.buf
        for ft in range(16):
            fns = []
            n = 0
            for dt in range(8):
                for hl in (self.b2h, self.b2l):
                    fns.append(lambda h, o=self.pg[0:128, 0:1],
                               l=self.PB[:, dt * FF + ft * 128: dt * FF + (ft + 1) * 128],
                               r=hl[:, dt:dt + 1], st=(n == 0), sp=(n == 15):
                               h.matmul(o, l, r, start=st, stop=sp))
                    n += 1
            self.mm_group(fns, reads=[b("PB"), b("b2h"), b("b2l")], writes=[b("pg")])
            self.dve("tensor_copy", self.hbias[:, ft:ft + 1], self.pg[0:128, 0:1],
                     reads=[b("pg")], writes=[b("hbias")])
        pcol = (self.p1c, self.p2c)[slot]
        pcolb = (b("p1c"), b("p2c"))[slot]
        self.ps_mode("half")
        for tch in range(8):
            for ft in range(16):
                pst, psb = self.ps_half(ft)
                fns = []
                for dt in range(8):
                    fns.append(lambda h, o=pst,
                               l=self.PB[:, dt * FF + ft * 128: dt * FF + (ft + 1) * 128],
                               r=self.PA[:, dt * T + tch * 512: dt * T + (tch + 1) * 512],
                               st=(dt == 0), sp=(dt == 7):
                               h.matmul(o, l, r, start=st, stop=sp))
                self.mm_group(fns, reads=[b("PB"), b("PA")], writes=[psb])
                if ft < 8:
                    hdst, hb = self.WOH[:, ft * 512:(ft + 1) * 512], b("WOH")
                else:
                    hdst, hb = self.PC[:, (ft - 8) * 512:(ft - 7) * 512], b("hTb")
                self.act(hdst, pst, AF.Gelu, bias=self.hbias[:, ft:ft + 1],
                         reads=[psb, b("hbias")], writes=[hb])
            for tl in range(4):
                tt = tch * 4 + tl
                in1, in1b = (self.xt0, self.xt1)[tt % 2], (b("xt0"), b("xt1"))[tt % 2]
                in2, in2b = self.xt2, b("xt2")
                ot, ob = (self.ot0, self.ot1)[tt % 2], (b("ot0"), b("ot1"))[tt % 2]
                rows = slice(tt * 128, (tt + 1) * 128)
                if slot == 0:
                    kb.dma(in1[:], xsrc_ap[rows, :], reads=[xsrc_buf] if xsrc_buf else [],
                           writes=[in1b])
                    kb.dma(in2[:], self.d_sB[rows, :], reads=[b("d_sB")], writes=[in2b])
                else:
                    kb.dma(in1[:], self.d_sA[rows, :], reads=[b("d_sA")], writes=[in1b])
                    kb.dma(in2[:], self.d_ya[rows, :], reads=[b("d_ya")], writes=[in2b])
                for dch in range(2):
                    pst, psb = self.ps_half(4 + tl * 2 + dch)
                    fns = []
                    rds = [b("WOH"), b("hTb"), b("fw2v"), b("qcA")]
                    for ft in range(16):
                        if ft < 8:
                            lsl = self.WOH[:, ft * 512 + tl * 128: ft * 512 + (tl + 1) * 128]
                        else:
                            lsl = self.PC[:, (ft - 8) * 512 + tl * 128: (ft - 8) * 512 + (tl + 1) * 128]
                        if ft < 12:
                            rsl = self.PD[:, ft * 1024 + dch * 512: ft * 1024 + (dch + 1) * 512]
                        else:
                            rsl = self.qcatA[:, (ft - 12) * 1024 + dch * 512:
                                             (ft - 12) * 1024 + (dch + 1) * 512]
                        fns.append(lambda h, o=pst, l=lsl, r=rsl,
                                   st=(ft == 0), sp=(ft == 15):
                                   h.matmul(o, l, r, start=st, stop=sp))
                    self.mm_group(fns, reads=rds, writes=[psb])
                    self.dve("tensor_scalar", ot[:, dch * 512:(dch + 1) * 512], pst,
                             pcol[:], None, ALU.mult, reads=[psb, pcolb], writes=[ob])
                if slot == 0:
                    kb.dma(self.d_ya[rows, :], ot[:], reads=[ob], writes=[b("d_ya")], q="sc")
                    self.dve("tensor_scalar", in2[:], in2[:], 0.5, None, ALU.mult,
                             reads=[in2b], writes=[in2b])
                    self.dve("tensor_tensor", in2[:], in2[:], in1[:], ALU.add,
                             reads=[in2b, in1b], writes=[in2b])
                    kb.dma(self.d_sA[rows, :], in2[:], reads=[in2b], writes=[b("d_sA")], q="sc")
                else:
                    self.dve("tensor_tensor", ot[:], ot[:], in2[:], ALU.add,
                             reads=[ob, in2b], writes=[ob])
                    self.dve("tensor_tensor", ot[:], ot[:], in1[:], ALU.add,
                             reads=[ob, in1b], writes=[ob])
                    kb.dma(dst_ap[rows, :], ot[:], reads=[ob], writes=[dst_buf], q="sc")
        self.ps_mode("full")

    def emit_mha_call(self, ci, xsrc_ap, xsrc_buf, dst_ap, dst_buf):
        kb, b = self.kb, self.buf
        gi = 0
        for m in range(M):
            self.dve("tensor_scalar", self.pbo[0:1, 0:1024],
                     self.bor[0:1, m * D:(m + 1) * D],
                     self.probs[0:1, m:m + 1], None, ALU.mult,
                     reads=[b("bor"), b("probs")], writes=[b("pbo")])
            self.dve("tensor_copy", self.pboh[0:1, 0:1024], self.pbo[0:1, 0:1024],
                     reads=[b("pbo")], writes=[b("pboh")])
            self.dve("tensor_tensor", self.pbol[0:1, 0:1024], self.pbo[0:1, 0:1024],
                     self.pboh[0:1, 0:1024], ALU.subtract,
                     reads=[b("pbo"), b("pboh")], writes=[b("pbol")])
            for half in range(2):
                for hp in range(half * 4, half * 4 + 4):
                    if self.emit_gen_ffn(m, hp):
                        return
                    if _CUT == "mha_g0":
                        return
                    if _CUT == "mha_g2" and hp == 2:
                        return
                    if _CUT == "mha_g3" and hp == 3:
                        return
                    ksrc = ((self.PD, 0, 64, T, b("PDk")), (self.PD, 64, 128, T, b("PDk")))
                    qsrc = ((self.PD, 0, 64, 0, b("PDq")), (self.PD, 64, 128, 0, b("PDq")))
                    gi = self.emit_attention(
                        hp, ksrc, qsrc, SC_FFN, lambda qch: 32, False,
                        self.probs[0:1, m:m + 1], b("probs"), (hp % 4) * T, gi)
                    if _CUT == "mha_a0":
                        return
                    if _CUT == "mha_a1" and hp == 1:
                        return
                    if _CUT == "mha_a2" and hp == 2:
                        return
                if _CUT == "mha_h0":
                    return
                if _CUT == "mha_h1" and half == 1:
                    return
                if _CUT == "mha_h2" and m == 1:
                    return
                srcs = {(0, 0): (self.d_aS, b("d_aS")), (0, 1): (self.d_sA, b("d_sA")),
                        (1, 0): (self.d_sB, b("d_sB")), (1, 1): (self.d_sA, b("d_sA"))}
                dsts = {(0, 0): (self.d_sA, b("d_sA")), (0, 1): (self.d_sB, b("d_sB")),
                        (1, 0): (self.d_sA, b("d_sA")), (1, 1): (self.d_sB, b("d_sB"))}
                sap, sbuf_ = srcs[(m, half)]
                dap, dbuf_ = dsts[(m, half)]
                wo_src = self.d_wo[m, half * 512:(half + 1) * 512, :].rearrange(
                    "(hh p) d -> p hh d", p=128)
                bias_rows = [(self.pboh, "pboh", 0), (self.pbol, "pbol", 0)] if half == 0 else None
                self.emit_proj_pass(wo_src, half, sap, sbuf_, dap, dbuf_,
                                    bias_rows, with_stats=(m == 1 and half == 1))
                if _CUT in ("mha_p0", "mha_w0"):
                    return
                if _CUT == "mha_p1" and (m, half) == (0, 1):
                    return
                if _CUT == "mha_p2" and (m, half) == (1, 0):
                    return
        if _CUT == f"mha{ci}_mo":
            self.emit_taps()
            return
        self.emit_ln(self.d_sB, b("d_sB"), self.g2c, fused=False, colsum=True,
                     stats_ready=True)
        self.emit_meancol(self.g2c, self.b2c, "g2c", "b2c")
        self.emit_router(NE, self.frwb, "frwb", self.frbr[0:1, 0:NE], "frbr")
        self.emit_topk()
        self.dve("tensor_scalar", self.t5[:], self.epv[0:1, 0:1], 0.5, None, ALU.mult,
                 reads=[b("epv")], writes=[b("t5")])
        self.bcast_row(self.t5[:], [b("t5")], self.p1c[:], b("p1c"), 4, 128, 1)
        self.dve("tensor_scalar", self.t6[:], self.epv[0:1, 1:2], 0.5, None, ALU.mult,
                 reads=[b("epv")], writes=[b("t6")])
        self.bcast_row(self.t6[:], [b("t6")], self.p2c[:], b("p2c"), 5, 128, 1)
        if _CUT == f"mha{ci}_router2":
            self.emit_taps()
            return
        kb.alias(b("fw2v"), [b("PDq"), b("PDk"), b("PDv")])
        kb.alias(b("qcA"), [b("qcA")])
        kb.alias(b("hTb"), [b("PCq"), b("PCk"), b("PCv")])
        self.emit_expert_select_gather(0)
        self.emit_expert_compute(0, ci, xsrc_ap, xsrc_buf, None, None)
        self.emit_expert_select_gather(1)
        self.emit_expert_compute(1, ci, xsrc_ap, xsrc_buf, dst_ap, dst_buf)
        kb.alias(b("PDq"), [b("fw2v")])
        kb.alias(b("PDk"), [b("fw2v")])
        kb.alias(b("PDv"), [b("fw2v")])
        for nm in ("PCq", "PCk", "PCv"):
            kb.alias(b(nm), [b("hTb")])
        vp_ap = self.sap(self.PD, 2 * T + 64, [[130, NTT], [65, 2]])
        self.dve("memset", vp_ap, 1.0, writes=[b("PDv")])

    def emit_mla(self):
        kb, b = self.kb, self.buf
        self.emit_ln(self.d_xs1, b("d_xs1"), None, fused=False, colsum=False)
        kb.pool_dma(self.PC[:, 0:1024].rearrange("p (dt c) -> p dt c", dt=8),
                    self.d_wkv.rearrange("(dt p) c -> p dt c", p=128), writes=[b("PCq")])
        kb.pool_dma(self.PC[:, 1024:2048], self.d_wkup, writes=[b("PCk")])
        kb.pool_dma(self.PC[:, 2048:3072], self.d_wvup, writes=[b("PCv")])
        kb.pool_dma(self.PC[:, 3072:3120], self.d_wkr, writes=[b("PCv")])
        kb.pool_dma(self.PC[:, 3120:3504].rearrange("p (dt c) -> p dt c", dt=8),
                    self.d_wqr.rearrange("(dt p) c -> p dt c", p=128), writes=[b("PCv")])
        for tch in range(8):
            fns = []
            for dt in range(8):
                fns.append(lambda h, o=self.pg[:, 0:512],
                           l=self.PC[:, dt * 128:(dt + 1) * 128],
                           r=self.PA[:, dt * T + tch * 512: dt * T + (tch + 1) * 512],
                           st=(dt == 0), sp=(dt == 7):
                           h.matmul(o, l, r, start=st, stop=sp))
            self.mm_group(fns, reads=[b("PCq"), b("PA")], writes=[b("pg")])
            self.dve("tensor_copy", self.ckvT[:, tch * 512:(tch + 1) * 512],
                     self.pg[:, 0:512], reads=[b("pg")], writes=[b("ckvT")])
        for (kind, dsts) in (("k", ((self.PD, 0), (self.PD, T))),
                             ("q", ((self.qcatA, 0), (self.WOH, 0)))):
            dbufs = (b("PDq"), b("PDk")) if kind == "k" else (b("qcA"), b("WOH"))
            for tch in range(8):
                cols = slice(tch * 512, (tch + 1) * 512)
                if kind == "k":
                    self.mm(self.pg[0:48, 0:512], self.PC[:, 3072:3120],
                            self.ckvT[:, cols], reads=[b("PCv"), b("ckvT")],
                            writes=[b("pg")])
                else:
                    fns = []
                    for dt in range(8):
                        fns.append(lambda h, o=self.pg[0:48, 0:512],
                                   l=self.PC[:, 3120 + dt * 48: 3120 + (dt + 1) * 48],
                                   r=self.PA[:, dt * T + tch * 512: dt * T + (tch + 1) * 512],
                                   st=(dt == 0), sp=(dt == 7):
                                   h.matmul(o, l, r, start=st, stop=sp))
                    self.mm_group(fns, reads=[b("PCv"), b("PA")], writes=[b("pg")])
                self.dve("tensor_copy", self.xt2[0:48, 0:512], self.pg[0:48, 0:512],
                         reads=[b("pg")], writes=[b("krp")])
                kb.dma(self.cs0[:], self.d_cosT[:, cols], writes=[b("cs0")])
                kb.dma(self.cs32[:], self.d_cosT[:, cols], writes=[b("cs32")])
                kb.dma(self.sn0[:], self.d_sinT[:, cols], writes=[b("sn0")])
                kb.dma(self.sn32[:], self.d_sinT[:, cols], writes=[b("sn32")])
                base = dsts[0][1]
                self.dve("tensor_tensor", self.rt1[:], self.krp1[0:16, :], self.cs0[:],
                         ALU.mult, reads=[b("krp"), b("cs0")], writes=[b("rt1")])
                self.dve("tensor_tensor", self.rt2[:], self.krp2[0:16, :], self.sn32[:],
                         ALU.mult, reads=[b("krp"), b("sn32")], writes=[b("rt2")])
                self.dve("tensor_tensor", dsts[0][0][64:80, base + tch * 512: base + tch * 512 + 512],
                         self.rt1[:], self.rt2[:], ALU.subtract,
                         reads=[b("rt1"), b("rt2")], writes=[dbufs[0]])
                self.dve("tensor_tensor", self.rt1[:], self.krp2[0:16, :], self.cs32[:],
                         ALU.mult, reads=[b("krp"), b("cs32")], writes=[b("rt1")])
                self.dve("tensor_tensor", self.rt2[:], self.krp1[0:16, :], self.sn0[:],
                         ALU.mult, reads=[b("krp"), b("sn0")], writes=[b("rt2")])
                self.dve("tensor_tensor", self.rote[0:16, :], self.rt1[:], self.rt2[:],
                         ALU.add, reads=[b("rt1"), b("rt2")], writes=[b("rote")])
                kb.dma(dsts[0][0][80:96, base + tch * 512: base + tch * 512 + 512],
                       self.rote[0:16, :], reads=[b("rote")], writes=[dbufs[0]])
                b2_, bs2 = dsts[1]
                kb.dma(b2_[64:96, bs2 + tch * 512: bs2 + tch * 512 + 512],
                       dsts[0][0][64:96, base + tch * 512: base + tch * 512 + 512],
                       reads=[dbufs[0]], writes=[dbufs[1]])
        gi = 0
        for half in range(2):
            if half == 1:
                kb.dma(self.WOH[64:96, :], self.qcatA[64:96, :],
                       reads=[b("qcA")], writes=[b("WOH")])
            for hp in range(half * 4, half * 4 + 4):
                kb.pool_dma(self.PC[:, 0:1024].rearrange("p (dt c) -> p dt c", dt=8),
                            self.d_wqa[:, hp * 128:(hp + 1) * 128].rearrange(
                                "(dt p) c -> p dt c", p=128), writes=[b("PCq")])
                self.ps_mode("half")
                rings = ((self.psA, 0, "psA_0"), (self.psA, 512, "psA_1"),
                         (self.psB, 0, "psB_0"), (self.psB, 512, "psB_1"))
                ri = 0
                for tch in range(8):
                    pt, lo, pnm = rings[ri % 4]
                    ri += 1
                    ptb = self.buf(pnm)
                    cols = slice(tch * 512, (tch + 1) * 512)
                    self.mm(pt[:, lo:lo + 512],
                            self.PC[:, 1024 + hp * 128: 1024 + (hp + 1) * 128],
                            self.ckvT[:, cols], reads=[b("PCk"), b("ckvT")], writes=[ptb])
                    self.dve("tensor_copy", self.PD[0:64, tch * 512:(tch + 1) * 512],
                             pt[0:64, lo:lo + 512], reads=[ptb], writes=[b("PDq")])
                    self.dve("tensor_copy", self.PD[0:64, T + tch * 512: T + (tch + 1) * 512],
                             pt[64:128, lo:lo + 512], reads=[ptb], writes=[b("PDk")])
                for tch in range(8):
                    pt, lo, pnm = rings[ri % 4]
                    ri += 1
                    ptb = self.buf(pnm)
                    fns = []
                    for dt in range(8):
                        fns.append(lambda h, o=pt[:, lo:lo + 512],
                                   l=self.PC[:, dt * 128:(dt + 1) * 128],
                                   r=self.PA[:, dt * T + tch * 512: dt * T + (tch + 1) * 512],
                                   st=(dt == 0), sp=(dt == 7):
                                   h.matmul(o, l, r, start=st, stop=sp))
                    self.mm_group(fns, reads=[b("PCq"), b("PA")], writes=[ptb])
                    self.dve("tensor_copy", self.qcatA[0:64, tch * 512:(tch + 1) * 512],
                             pt[0:64, lo:lo + 512], reads=[ptb], writes=[b("qcA")])
                    self.dve("tensor_copy", self.WOH[0:64, tch * 512:(tch + 1) * 512],
                             pt[64:128, lo:lo + 512], reads=[ptb], writes=[b("WOH")])
                for tt in range(NTT):
                    pt, lo, pnm = rings[ri % 4]
                    ri += 1
                    ptb = self.buf(pnm)
                    fns = [lambda h, o=pt[:, lo:lo + 128],
                           l=self.ckvT[:, tt * 128:(tt + 1) * 128],
                           r=self.PC[:, 2048 + hp * 128: 2048 + (hp + 1) * 128]:
                           h.matmul(o, l, r, start=True, stop=True)]
                    self.mm_group(fns, reads=[b("ckvT"), b("PCv")], writes=[ptb])
                    dstv = self.sap(self.PD, 2 * T + tt * 130, [[65, 2], [1, 64]])
                    self.dve("tensor_copy", dstv,
                             pt[:, lo:lo + 128].rearrange("p (g c) -> p g c", c=64),
                             reads=[ptb], writes=[b("PDv")])
                ksrc = ((self.PD, 0, 96, 0, b("PDq")), (self.PD, 0, 96, T, b("PDk")))
                qsrc = ((self.qcatA, 0, 96, 0, b("qcA")), (self.WOH, 0, 96, 0, b("WOH")))
                gi = self.emit_attention(hp, ksrc, qsrc, SC_MLA,
                                         lambda qch: 4 * qch + 4, True,
                                         None, None, (hp % 4) * T, gi)
            wo_src = self.d_woa[half * 512:(half + 1) * 512, :].rearrange(
                "(hh p) d -> p hh d", p=128)
            if half == 0:
                self.emit_proj_pass(wo_src, half, self.d_xs1, b("d_xs1"),
                                    self.d_sA, b("d_sA"), None, False)
            else:
                self.emit_proj_pass(wo_src, half, self.d_sA, b("d_sA"),
                                    self.d_xs2, b("d_xs2"), None, False)

    def emit_taps(self):
        b = self.buf
        if "dbg_PA" in _DBG:
            self.kb.dma(self.d_dbg_PA, self.PA[:], reads=[b("PA")], writes=[b("d_dbg_PA")])
        if "dbg_PB" in _DBG:
            self.kb.dma(self.d_dbg_PB, self.PB[:], reads=[b("PB")], writes=[b("d_dbg_PB")])
        if "dbg_PD" in _DBG:
            self.kb.dma(self.d_dbg_PD, self.PD[:], reads=[b("PDq"), b("PDk"), b("PDv")],
                        writes=[b("d_dbg_PD")])
        if "dbg_qc" in _DBG:
            self.kb.dma(self.d_dbg_qc, self.qcatA[:], reads=[b("qcA")], writes=[b("d_dbg_qc")])
        if "dbg_WOH" in _DBG:
            self.kb.dma(self.d_dbg_WOH, self.WOH[:], reads=[b("WOH")], writes=[b("d_dbg_WOH")])
        if "dbg_ckv" in _DBG:
            self.kb.dma(self.d_dbg_ckv, self.ckvT[:], reads=[b("ckvT")], writes=[b("d_dbg_ckv")])
        if "dbg_row" in _DBG:
            k = self.kb
            k.dma(self.d_dbg_row[:, 0:8], self.csT[:], reads=[b("csT")], writes=[b("d_dbg_row")])
            k.dma(self.d_dbg_row[:, 8:16], self.mnc[:], reads=[b("mnc")], writes=[b("d_dbg_row")])
            k.dma(self.d_dbg_row[0:1, 16:16 + NE], self.rrow[0:1, 0:NE],
                  reads=[b("rrow")], writes=[b("d_dbg_row")])
            k.dma(self.d_dbg_row[0:1, 24:26], self.probs[0:1, 0:2],
                  reads=[b("probs")], writes=[b("d_dbg_row")])
            k.dma(self.d_dbg_row[0:1, 26:28], self.eidx[0:1, 0:2],
                  reads=[b("eidx")], writes=[b("d_dbg_row")])
            k.dma(self.d_dbg_row[0:1, 28:30], self.epv[0:1, 0:2],
                  reads=[b("epv")], writes=[b("d_dbg_row")])
            k.dma(self.d_dbg_row[:, 32:64], self.stats[:, 0:32],
                  reads=[b("stats")], writes=[b("d_dbg_row")])

    def emit(self):
        b = self.buf
        self.emit_setup()
        self.emit_ln(self.d_x0, None, self.g1c, fused=True, colsum=True)
        if _CUT == "ln1":
            self.emit_taps()
            return
        self.emit_meancol(self.g1c, self.b1c, "g1c", "b1c")
        self.emit_router(M, self.mrwb, "mrwb", self.mrbr[0:1, 0:M], "mrbr")
        self.emit_probs2()
        if _CUT == "router1":
            self.emit_taps()
            return
        self.emit_mha_call(0, self.d_x0, None, self.d_xs1, b("d_xs1"))
        if _CUT is not None and _CUT.startswith(("mha_", "mha0", "g2", "g3")):
            self.emit_taps()
            return
        self.emit_mla()
        if _CUT == "mla":
            self.emit_taps()
            return
        self.emit_ln(self.d_xs2, b("d_xs2"), self.g1c, fused=True, colsum=True)
        self.emit_meancol(self.g1c, self.b1c, "g1c", "b1c")
        self.emit_router(M, self.mrwb, "mrwb", self.mrbr[0:1, 0:M], "mrbr")
        self.emit_probs2()
        self.emit_mha_call(1, self.d_xs2, b("d_xs2"), self.d_y, b("d_y"))
        yb = b("d_y")
        for tok in list(yb.w.values()):
            self.kb._wait("sp", tok)


def kernel(**inputs):
    pk = _hostpack(inputs)
    x = np.asarray(inputs["L0_attn_x"], dtype=np.float32)
    nc = build_program()
    in_maps = []
    for c in range(8):
        m = dict(pk)
        m["x0"] = np.ascontiguousarray(x[c >> 1])
        in_maps.append(m)
    res = run_bass_kernel_spmd(nc, in_maps, core_ids=list(range(8)))
    out = np.stack([np.asarray(res.results[2 * bb]["y"], dtype=np.float32)
                    for bb in range(4)], axis=0)
    return out


# version 10 (final)
# speedup vs baseline: 1.1733x; 1.1733x over previous
import numpy as np
import ml_dtypes
from contextlib import ExitStack

import concourse.bass as bass
import concourse.mybir as mybir
from concourse.bass_utils import run_bass_kernel_spmd

F32 = mybir.dt.float32
BF16 = mybir.dt.bfloat16
U32 = mybir.dt.uint32
AF = mybir.ActivationFunctionType
ALU = mybir.AluOpType
AX = mybir.AxisListType

T, D = 4096, 1024
NTT = T // 128
NCH = T // 512
NH, HD, DC, DR = 16, 64, 128, 32
M, NE, TOPK = 2, 8, 2
FF = 2048
EPS_OUT, EPS_IN = 1e-5, 1e-6
SC_FFN = 1.0 / 8.0
SC_MLA = 1.0 / float(np.sqrt(96.0))

import os as _os
_NOBC = _os.environ.get("KNOBC", "0") == "1"
_NOVB = _os.environ.get("KNOVB", "0") == "1"
_CUT = None
_DBG = {}


class _TV:
    def __init__(self, h, p0, np_, c0, w):
        self.h, self.p0, self.np_, self.c0, self.w = h, p0, np_, c0, w

    def __getitem__(self, idx):
        if not isinstance(idx, tuple):
            idx = (slice(None), slice(None))
        rs, cs = idx
        r0 = rs.start or 0
        r1 = rs.stop if rs.stop is not None else self.np_
        a = cs.start or 0
        bb = cs.stop if cs.stop is not None else self.w
        return self.h[self.p0 + r0: self.p0 + r1, self.c0 + a: self.c0 + bb]


class _RV:
    def __init__(self, h, p0, c0, w):
        self.h, self.p0, self.c0, self.w = h, p0, c0, w

    def __getitem__(self, idx):
        rows, cols = idx
        assert rows.start in (0, None) and rows.stop in (1, None)
        a = cols.start or 0
        bb = cols.stop if cols.stop is not None else self.w
        return self.h[self.p0:self.p0 + 1, self.c0 + a: self.c0 + bb]


class Buf:
    __slots__ = ("name", "w", "r")

    def __init__(self, name):
        self.name = name
        self.w = {}
        self.r = {}


class KB:
    ENG = ("pe", "act", "dve", "pool", "sp")

    def __init__(self, nc, stk):
        self.nc = nc
        self.stk = stk
        self._nsem = 0
        self.ops = {e: [] for e in self.ENG}
        self.sem_cur = {}
        self.sem_val = {}
        for e in self.ENG:
            self._rotate_engine_sem(e)
        self.sp_ring = [self._mksem(f"D{i}") for i in range(24)]
        self.sp_cnt = [0] * 24
        self.sp_i = 0
        self.pool_ring = [self._mksem(f"P{i}") for i in range(28)]
        self.pool_cnt = [0] * 28
        self.pool_i = 0
        self.sc_ring = [self._mksem(f"C{i}") for i in range(6)]
        self.sc_cnt = [0] * 6
        self.sc_i = 0
        self.wm = {e: {} for e in self.ENG}
        self._nsem = 0

    def _mksem(self, name):
        self._nsem += 1
        return self.stk.enter_context(self.nc.semaphore(name))

    def _rotate_engine_sem(self, e):
        self.sem_cur[e] = self._mksem(f"S{e}{self.sem_val.get(e, 0)}x")
        self.sem_val[e] = 0

    def _wait(self, e, tok):
        sem, val = tok
        wm = self.wm[e]
        k = id(sem)
        if wm.get(k, 0) >= val:
            return
        wm[k] = val
        self.ops[e].append(lambda h, s=sem, v=val: h.wait_ge(s, v))

    def _put(self, d, tok):
        k = id(tok[0])
        old = d.get(k)
        if old is None or old[1] < tok[1]:
            d[k] = tok

    def _deps(self, e, reads, writes):
        for b in reads:
            for tok in b.w.values():
                self._wait(e, tok)
        for b in writes:
            for tok in b.w.values():
                self._wait(e, tok)
            for tok in b.r.values():
                self._wait(e, tok)

    def _record(self, tok, reads, writes):
        for b in reads:
            self._put(b.r, tok)
        for b in writes:
            self._put(b.w, tok)
            b.r = {}

    def op(self, e, fn, reads=(), writes=()):
        self._deps(e, reads, writes)
        if self.sem_val[e] >= 30000:
            self._rotate_engine_sem(e)
        sem = self.sem_cur[e]
        self.sem_val[e] += 1
        val = self.sem_val[e]
        self.ops[e].append(lambda h, f=fn, s=sem: f(h).then_inc(s, 1))
        tok = (sem, val)
        self._record(tok, reads, writes)
        return tok

    def op_group(self, e, fns, reads=(), writes=()):
        self._deps(e, reads, writes)
        if self.sem_val[e] >= 30000:
            self._rotate_engine_sem(e)
        for fn in fns[:-1]:
            self.ops[e].append(lambda h, f=fn: f(h))
        sem = self.sem_cur[e]
        self.sem_val[e] += 1
        val = self.sem_val[e]
        self.ops[e].append(lambda h, f=fns[-1], s=sem: f(h).then_inc(s, 1))
        tok = (sem, val)
        self._record(tok, reads, writes)
        return tok

    def dma(self, out_ap, in_ap, reads=(), writes=(), q="sp"):
        if q == "sc":
            e, ring, cnt = "act", self.sc_ring, self.sc_cnt
            i = self.sc_i
            self.sc_i = (self.sc_i + 1) % len(ring)
        else:
            e, ring, cnt = "sp", self.sp_ring, self.sp_cnt
            i = self.sp_i
            self.sp_i = (self.sp_i + 1) % len(ring)
        self._deps(e, reads, writes)
        sem = ring[i]
        if cnt[i] > 0:
            self._wait(e, (sem, 16 * cnt[i]))
        cnt[i] += 1
        val = 16 * cnt[i]
        self.ops[e].append(
            lambda h, o=out_ap, ii=in_ap, s=sem: h.dma_start(out=o, in_=ii).then_inc(s, 16))
        tok = (sem, val)
        self._record(tok, reads, writes)
        return tok

    def pool_dma(self, out_ap, in_ap, reads=(), writes=(), indirect=None):
        self._deps("pool", reads, writes)
        i = self.pool_i
        self.pool_i = (self.pool_i + 1) % len(self.pool_ring)
        sem = self.pool_ring[i]
        if self.pool_cnt[i] > 0:
            self._wait("pool", (sem, 16 * self.pool_cnt[i]))
        self.pool_cnt[i] += 1
        val = 16 * self.pool_cnt[i]
        if indirect is None:
            self.ops["pool"].append(
                lambda h, o=out_ap, ii=in_ap, s=sem: h.dma_start(out=o, in_=ii).then_inc(s, 16))
        else:
            self.ops["pool"].append(
                lambda h, o=out_ap, ii=in_ap, s=sem, off=indirect:
                    h.indirect_dma_start(out=o, out_offset=None, in_=ii, in_offset=off)
                    .then_inc(s, 16))
        tok = (sem, val)
        self._record(tok, reads, writes)
        return tok

    def wm_clear(self, sem):
        k = id(sem)
        for e in self.ENG:
            self.wm[e].pop(k, None)

    def barrier_wait(self, e, toks):
        for t in toks:
            if t is not None:
                self._wait(e, t)

    def alias(self, newbuf, oldbufs):
        for ob in oldbufs:
            if ob is newbuf:
                continue
            for tok in ob.w.values():
                self._put(newbuf.r, tok)
            for tok in ob.r.values():
                self._put(newbuf.r, tok)

    def finalize(self, block):
        for e, attr in (("sp", "sync"), ("pe", "tensor"), ("act", "scalar"),
                        ("dve", "vector"), ("pool", "gpsimd")):
            ops = self.ops[e]
            dec = getattr(block, attr)

            def mk(ops_list):
                def body(h):
                    for f in ops_list:
                        f(h)
                return body
            dec(mk(ops))


def _hostpack(inputs):
    f32 = np.float32
    g = lambda x: np.ascontiguousarray(np.asarray(x, dtype=f32))
    d = {}
    d["wkv"] = g(inputs["L0_attn_W_kv_down"])
    d["wkup"] = g(inputs["L0_attn_W_k_up"])
    d["wvup"] = g(inputs["L0_attn_W_v_up"])
    d["wqa"] = g(inputs["L0_attn_W_q"])
    wqr = g(inputs["L0_attn_W_q_rope"])
    wkr = g(inputs["L0_attn_W_k_rope"])
    wkr48 = np.zeros((128, 48), np.float32)
    wkr48[:, 0:16] = wkr[:, 0:16]; wkr48[:, 32:48] = wkr[:, 16:32]
    d["wkr"] = wkr48
    wqr48 = np.zeros((1024, 48), np.float32)
    wqr48[:, 0:16] = wqr[:, 0:16]; wqr48[:, 32:48] = wqr[:, 16:32]
    d["wqr"] = wqr48
    d["woa"] = g(inputs["L0_attn_W_o"])
    for nm, key in (("g1", "L0_ffn_ln1_g"), ("b1", "L0_ffn_ln1_b"),
                    ("g2", "L0_ffn_ln2_g"), ("b2", "L0_ffn_ln2_b")):
        d[nm] = g(inputs[key]).reshape(8, 128).T.copy()
    d["wq"] = g(inputs["L0_ffn_mha_wq"]).reshape(M, D, NH * HD)
    d["wk"] = g(inputs["L0_ffn_mha_wk"]).reshape(M, D, NH * HD)
    d["wv"] = g(inputs["L0_ffn_mha_wv"]).reshape(M, D, NH * HD)
    d["wo"] = g(inputs["L0_ffn_mha_wo"]).reshape(M, NH * HD, D)
    bq = g(inputs["L0_ffn_mha_bq"]).reshape(M, NH * HD)
    bk = g(inputs["L0_ffn_mha_bk"]).reshape(M, NH * HD)
    d["bqc"] = np.concatenate([bq[m].reshape(8, 128).T for m in range(M)], axis=1).copy()
    d["bkc"] = np.concatenate([bk[m].reshape(8, 128).T for m in range(M)], axis=1).copy()
    d["bv"] = g(inputs["L0_ffn_mha_bv"]).reshape(M, NH * HD)
    d["bo"] = g(inputs["L0_ffn_mha_bo"])
    d["mrw"] = g(inputs["L0_ffn_mha_rw"]).reshape(8, 128, M).transpose(1, 0, 2).reshape(128, 8 * M).copy()
    d["mrb"] = g(inputs["L0_ffn_mha_rb"]).reshape(1, M)
    d["frw"] = g(inputs["L0_ffn_ffn_rw"]).reshape(8, 128, NE).transpose(1, 0, 2).reshape(128, 8 * NE).copy()
    d["frb"] = g(inputs["L0_ffn_ffn_rb"]).reshape(1, NE)
    d["fw1"] = g(inputs["L0_ffn_fw1"]).reshape(NE * D, FF)
    d["fw2"] = g(inputs["L0_ffn_fw2"]).reshape(NE * FF, D)
    d["ident"] = np.eye(128, dtype=f32).astype(ml_dtypes.bfloat16)
    inv_freq = 1.0 / 10000.0 ** (np.arange(0, DR, 2, dtype=np.float64) / DR)
    tt = np.arange(T, dtype=np.float64)
    fr = np.outer(tt, inv_freq)
    d["cosT"] = np.cos(fr).T.astype(f32).copy()
    d["sinT"] = np.sin(fr).T.astype(f32).copy()
    mk = (np.arange(896)[None, :] >= (384 + np.arange(128)[:, None])).astype(f32)
    d["dmask"] = mk.astype(ml_dtypes.bfloat16)
    return d


def build_program():
    nc = bass.Bass("TRN2", target_bir_lowering=False, debug=False)
    stk = ExitStack()
    with stk:
        kb = KB(nc, stk)
        mg = _Mega(nc, kb, stk)
        mg.declare()
        mg.emit()
        block = stk.enter_context(nc.Block())
        kb.finalize(block)
    return nc


class _Mega:
    def __init__(self, nc, kb, stk):
        self.nc, self.kb, self.stk = nc, kb, stk
        self.bufs = {}

    def buf(self, name):
        b = self.bufs.get(name)
        if b is None:
            b = Buf(name)
            self.bufs[name] = b
        return b

    def sbuf(self, name, shape, dtype):
        h = self.stk.enter_context(self.nc.sbuf_tensor(name, shape, dtype))
        setattr(self, name, h)
        self.buf(name)
        return h

    def psum(self, name, shape, dtype):
        h = self.stk.enter_context(self.nc.psum_tensor(name, shape, dtype))
        setattr(self, name, h)
        self.buf(name)
        return h

    def dram(self, name, shape, dtype, kind="ExternalInput"):
        if name in _DBG and kind == "Internal":
            kind = "ExternalOutput"
        t = self.nc.dram_tensor(name, shape, dtype, kind=kind)
        ap = t.ap()
        setattr(self, "d_" + name, ap)
        if kind != "ExternalInput":
            self.buf("d_" + name)
        return ap

    def declare(self):
        dr, sb, ps = self.dram, self.sbuf, self.psum
        dr("x0", [T, D], F32)
        dr("wkv", [D, DC], F32); dr("wkup", [DC, D], F32); dr("wvup", [DC, D], F32)
        dr("wqa", [D, D], F32); dr("wqr", [D, 48], F32); dr("wkr", [DC, 48], F32)
        dr("woa", [D, D], F32)
        for nm in ("g1", "b1", "g2", "b2"):
            dr(nm, [128, 8], F32)
        dr("wq", [M, D, NH * HD], F32); dr("wk", [M, D, NH * HD], F32)
        dr("wv", [M, D, NH * HD], F32); dr("wo", [M, NH * HD, D], F32)
        dr("bqc", [128, 16], F32); dr("bkc", [128, 16], F32)
        dr("bv", [M, NH * HD], F32); dr("bo", [M, D], F32)
        dr("mrw", [128, 8 * M], F32); dr("mrb", [1, M], F32)
        dr("frw", [128, 8 * NE], F32); dr("frb", [1, NE], F32)
        dr("fw1", [NE * D, FF], F32); dr("fw2", [NE * FF, D], F32)
        dr("ident", [128, 128], BF16)
        dr("cosT", [16, T], F32); dr("sinT", [16, T], F32)
        dr("dmask", [128, 896], BF16)
        dr("y", [T, D], F32, kind="ExternalOutput")
        for nm, shp, dt_ in (("dbg_PA", [128, 8 * T], BF16), ("dbg_PB", [128, 4 * T], BF16),
                             ("dbg_row", [128, 64], F32), ("dbg_PD", [128, 2 * T + NTT * 130], BF16),
                             ("dbg_qc", [128, 4096], BF16), ("dbg_WOH", [128, 4096], BF16),
                             ("dbg_ckv", [128, T], BF16)):
            if nm in _DBG:
                dr(nm, shp, dt_, kind="ExternalOutput")
        dr("sA", [T, D], F32, kind="Internal")
        dr("sB", [T, D], F32, kind="Internal")
        dr("xs1", [T, D], F32, kind="Internal")
        dr("xs2", [T, D], F32, kind="Internal")
        dr("ya", [T, D], F32, kind="Internal")
        dr("bcd", [8, 512], F32, kind="Internal")
        dr("aS", [T, D], F32, kind="Internal")

        sb("PA", [128, 8 * T], BF16)
        sb("PB", [128, 4 * T], BF16)
        sb("PD", [128, 2 * T + NTT * 130], BF16)
        for nm in ("PDq", "PDk", "PDv", "fw2v", "qcA"):
            self.buf(nm)
        sb("qcatA", [128, 4096], BF16)
        sb("PC", [128, 4096], BF16)
        sb("WOH", [128, 4096], BF16)
        sb("ckvT", [128, T], BF16)
        for i in range(3):
            sb(f"xt{i}", [128, D], F32)
        sb("ot0", [128, D], F32); sb("ot1", [128, D], F32)
        sb("zr0", [128, D], BF16); sb("zr1", [128, D], BF16)
        self.pbo = _RV(self.xt2, 0, 0, 1024)
        self.bufs["pbo"] = self.buf("xt2")
        self.krp1 = _TV(self.xt2, 0, 16, 0, 512)
        self.krp2 = _TV(self.xt2, 32, 16, 0, 512)
        self.rt1 = _TV(self.xt1, 64, 16, 0, 512)
        self.rt2 = _TV(self.ot0, 64, 16, 0, 512)
        self.cs0 = _TV(self.ot0, 0, 16, 0, 512)
        self.sn32 = _TV(self.ot0, 32, 16, 0, 512)
        self.sn0 = _TV(self.xt1, 0, 16, 0, 512)
        self.cs32 = _TV(self.xt1, 32, 16, 0, 512)
        self.rote = _TV(self.zr0, 64, 16, 0, 512)
        self.bufs["krp"] = self.buf("xt2")
        self.bufs["krp1"] = self.buf("xt2")
        self.bufs["krp2"] = self.buf("xt2")

        for nm in ("cs0", "sn32", "rt2"):
            self.bufs[nm] = self.buf("ot0")
        for nm in ("sn0", "cs32", "rt1"):
            self.bufs[nm] = self.buf("xt1")
        self.bufs["rote"] = self.buf("zr0")
        for i in range(3):
            sb(f"wT{i}", [128, 1024], BF16)
        self.craw0 = _TV(self.zr1, 0, 128, 0, 512)
        self.craw1 = _TV(self.zr1, 0, 128, 512, 512)
        self.bufs["craw0"] = self.buf("zr1"); self.bufs["craw1"] = self.buf("zr1")
        sb("rb0", [64, 512], F32); sb("rb1", [64, 512], F32)
        self.rz0 = _RV(self.ot1, 0, 0, 512)
        self.rz1 = _RV(self.ot1, 32, 0, 512)
        self.bufs["rz0"] = self.buf("ot1"); self.bufs["rz1"] = self.buf("ot1")
        sb("stats", [128, 64], F32); sb("st6a", [128, 12], F32); sb("st6b", [128, 12], F32)
        sb("scomb", [128, 32], F32)
        sb("lnw0", [128, 32], F32); sb("lnw1", [128, 32], F32)
        sb("lnw2", [128, 32], F32); sb("lnw3", [128, 32], F32)

        sb("io8r_u", [1, 8], U32); sb("io8r", [1, 8], F32)
        sb("sident", [128, 128], BF16)
        sb("ones_col", [128, 1], BF16); sb("onesR", [128, 128], BF16)
        sb("g1c", [128, 8], F32); sb("b1c", [128, 8], F32)
        sb("g2c", [128, 8], F32); sb("b2c", [128, 8], F32)
        sb("b1h", [128, 8], BF16); sb("b1l", [128, 8], BF16)
        sb("b2h", [128, 8], BF16); sb("b2l", [128, 8], BF16)
        sb("bqcs", [128, 16], F32); sb("bkcs", [128, 16], F32)
        sb("ROWA", [128, 2048], F32)
        sb("ROWC", [128, 2048], BF16)
        sb("VBR", [128, 384], BF16)
        sb("VBF", [128, 256], F32)
        self.bvr = _RV(self.ROWA, 0, 0, 2048)
        self.bor = _RV(self.ROWA, 32, 0, 2048)
        self.pboh = _RV(self.ROWC, 0, 0, 1024)
        self.pbol = _RV(self.ROWC, 0, 1024, 1024)
        for nm in ("bvr", "bor", "pboh", "pbol"):
            self.buf(nm)
        sb("mrwb", [128, 8 * M], BF16); sb("frwb", [128, 8 * NE], BF16)
        sb("mrbr", [1, M], F32); sb("frbr", [1, NE], F32)
        sb("dmk", [128, 896], BF16)
        self.csum = _RV(self.ROWA, 64, 0, 1024)
        self.csbf = _RV(self.ROWC, 64, 0, 1024)
        self.buf("csum"); self.buf("csbf")
        sb("csT", [128, 8], F32); sb("mnc", [128, 8], F32); sb("mncb", [128, 8], BF16)
        sb("rrow", [1, NE], F32)
        sb("probs", [1, M], F32)
        sb("t0", [1, 8], F32); sb("t1", [1, 8], F32); sb("t2", [1, 8], F32)
        sb("t3", [1, 1], F32); sb("t4", [1, 1], F32); sb("t5", [1, 1], F32)
        sb("t6", [1, 1], F32); sb("t7", [1, 1], F32)
        sb("eidx", [1, 2], F32)
        sb("epv", [1, 2], F32)
        sb("biasq", [128, 1], F32); sb("biask", [128, 1], F32)
        sb("hbias", [128, 16], F32)
        sb("io8f", [128, 8], F32); sb("io16f", [128, 16], F32)
        sb("io8u", [128, 8], U32); sb("io16u", [128, 16], U32)
        sb("ii1", [128, 8], U32); sb("ii2", [128, 16], U32)
        sb("iif1", [128, 8], F32); sb("iif2", [128, 16], F32)
        for i in range(4):
            sb(f"bcc{i}", [128, 1], F32)
        sb("p1c", [128, 1], F32); sb("p2c", [128, 1], F32)

        ps("psA", [128, 1024], F32)
        ps("psB", [128, 1024], F32)
        for nm in ("psA_0", "psA_1", "psB_0", "psB_1"):
            self.buf(nm)
        ps("pc0", [128, 512], F32)
        ps("pc1", [128, 512], F32)
        ps("pg", [128, 512], F32)
        ps("ptr", [128, 1024], BF16)

    def mm(self, out_ap, lhsT_ap, rhs_ap, reads, writes, start=True, stop=True):
        return self.kb.op("pe", lambda h, o=out_ap, l=lhsT_ap, r=rhs_ap, st=start, sp=stop:
                          h.matmul(o, l, r, start=st, stop=sp), reads=reads, writes=writes)

    def mm_group(self, fns, reads, writes):
        return self.kb.op_group("pe", fns, reads=reads, writes=writes)

    def act(self, out_ap, in_ap, func, reads, writes, bias=None, scale=None, accum=None):
        kw = {}
        if bias is not None:
            kw["bias"] = bias
        if scale is not None:
            kw["scale"] = scale
        if accum is not None:
            kw["accum_out"] = accum
        return self.kb.op("act", lambda h, o=out_ap, i=in_ap, f=func, k=dict(kw):
                          h.activation(o, i, f, **k), reads=reads, writes=writes)

    def dve(self, meth, *args, reads=(), writes=(), **kw):
        return self.kb.op("dve", lambda h, m=meth, a=args, k=dict(kw):
                          getattr(h, m)(*a, **k), reads=reads, writes=writes)

    def pool(self, meth, *args, reads=(), writes=(), **kw):
        return self.kb.op("pool", lambda h, m=meth, a=args, k=dict(kw):
                          getattr(h, m)(*a, **k), reads=reads, writes=writes)

    def bcast_row(self, src_row_ap, src_reads, dst_ap, dst_buf, slot, npart, width):
        kb = self.kb
        slot_buf = self.buf(f"bcd{slot}")
        dst_dram = bass.AP(self.d_bcd.tensor, slot * 512, [[1, 1], [1, width]])
        kb.dma(dst_dram, src_row_ap, reads=src_reads, writes=[slot_buf])
        bc = bass.AP(self.d_bcd.tensor, slot * 512, [[0, npart], [1, width]])
        return kb.dma(dst_ap, bc, reads=[slot_buf], writes=[dst_buf])

    def sap(self, handle, offset, free_dims, npart=128):
        base = handle[:]
        pitch = base.ap[0][0]
        return bass.AP(base.tensor, offset, [[pitch, npart]] + [list(d) for d in free_dims])

    def ps_mode(self, mode):
        if mode == getattr(self, "_psmode", None):
            return
        self._psmode = mode
        kb = self.kb
        if mode == "half":
            for nm in ("A", "B"):
                f = self.buf("ps" + nm)
                kb.alias(self.buf(f"ps{nm}_0"), [f])
                kb.alias(self.buf(f"ps{nm}_1"), [f])
        else:
            for nm in ("A", "B"):
                kb.alias(self.buf("ps" + nm),
                         [self.buf(f"ps{nm}_0"), self.buf(f"ps{nm}_1")])

    def ps_half(self, i, w=512):
        t = (self.psA, self.psA, self.psB, self.psB)[i % 4]
        lo = (0, 512, 0, 512)[i % 4]
        nm = ("psA_0", "psA_1", "psB_0", "psB_1")[i % 4]
        return t[:, lo:lo + w], self.buf(nm)

    def emit_setup(self):
        kb, b = self.kb, self.buf
        kb.dma(self.sident[:], self.d_ident, writes=[b("sident")])
        kb.dma(self.dmk[:], self.d_dmask, writes=[b("dmk")])
        for nm in ("g1", "b1", "g2", "b2"):
            kb.dma(getattr(self, nm + "c")[:], getattr(self, "d_" + nm),
                   writes=[b(nm + "c")])
        kb.dma(self.bqcs[:], self.d_bqc, writes=[b("bqcs")])
        kb.dma(self.bkcs[:], self.d_bkc, writes=[b("bkcs")])
        for m in range(M):
            kb.dma(self.bvr[0:1, m * 1024:(m + 1) * 1024], self.d_bv[m:m + 1, :],
                   writes=[b("bvr")])
            kb.dma(self.bor[0:1, m * 1024:(m + 1) * 1024], self.d_bo[m:m + 1, :],
                   writes=[b("bor")])
        kb.dma(self.stats[:, 0:16], self.d_mrw, writes=[b("stats")])
        self.dve("tensor_copy", self.mrwb[:], self.stats[:, 0:16],
                 reads=[b("stats")], writes=[b("mrwb")])
        kb.dma(self.stats[:, 0:64], self.d_frw, writes=[b("stats")])
        self.dve("tensor_copy", self.frwb[:], self.stats[:, 0:64],
                 reads=[b("stats")], writes=[b("frwb")])
        kb.dma(self.mrbr[:], self.d_mrb, writes=[b("mrbr")])
        kb.dma(self.frbr[:], self.d_frb, writes=[b("frbr")])
        for nm in ("b1", "b2"):
            c, h_, l_ = b(nm + "c"), b(nm + "h"), b(nm + "l")
            hi, lo = getattr(self, nm + "h"), getattr(self, nm + "l")
            self.dve("tensor_copy", hi[:], getattr(self, nm + "c")[:],
                     reads=[c], writes=[h_])
            self.dve("tensor_tensor", lo[:], getattr(self, nm + "c")[:], hi[:],
                     ALU.subtract, reads=[c, h_], writes=[l_])
        for t in (self.rrow, self.eidx, self.epv, self.probs, self.stats, self.csT, self.mnc):
            self.dve("memset", t[:], 0.0, writes=[self.buf("stats")])
        self.dve("memset", self.ones_col[:], 1.0, writes=[b("ones_col")])
        self.dve("memset", self.onesR[:], 1.0, writes=[b("onesR")])
        vp_ap = self.sap(self.PD, 2 * T + 64, [[130, NTT], [65, 2]])
        self.dve("memset", vp_ap, 1.0, writes=[b("PDv")])
        self.pool("iota", self.io8u[:], [[128, 8]], base=0, channel_multiplier=1,
                  writes=[b("io8u")])
        self.pool("iota", self.io16u[:], [[128, 16]], base=0, channel_multiplier=1,
                  writes=[b("io16u")])
        self.pool("iota", self.io8r_u[:], [[1, 8]], base=0, channel_multiplier=0,
                  writes=[b("io8r_u")])
        self.dve("tensor_copy", self.io8f[:], self.io8u[:], reads=[b("io8u")], writes=[b("io8f")])
        self.dve("tensor_copy", self.io16f[:], self.io16u[:], reads=[b("io16u")], writes=[b("io16f")])
        self.dve("tensor_copy", self.io8r[:], self.io8r_u[:], reads=[b("io8r_u")], writes=[b("io8r")])

    def emit_ln(self, src_ap, src_buf, gcol, fused, colsum, stats_ready=False):
        kb, b = self.kb, self.buf
        xts = [self.xt0, self.xt1, self.xt2]
        xbs = [b("xt0"), b("xt1"), b("xt2")]
        if not stats_ready:
            for tt in range(NTT):
                xt, xb = xts[tt % 3], xbs[tt % 3]
                rd = [src_buf] if src_buf else []
                kb.dma(xt[:], src_ap[tt * 128:(tt + 1) * 128, :], reads=rd, writes=[xb])
                st6 = (self.st6a, self.st6b)[tt % 2]
                s6b = (b("st6a"), b("st6b"))[tt % 2]
                self.dve("bn_stats", st6[:, 0:6], xt[:, 0:512], reads=[xb], writes=[s6b])
                self.dve("bn_stats", st6[:, 6:12], xt[:, 512:1024], reads=[xb], writes=[s6b])
                self.dve("bn_aggr", self.stats[:, 2 * tt:2 * tt + 2], st6[:, 0:12],
                         reads=[s6b], writes=[b("stats")])
        varv = self.sap(self.stats, 1, [[2, 32]])
        sb_, ssb = b("stats"), b("scomb")
        w0, w1, w2, w3 = b("lnw0"), b("lnw1"), b("lnw2"), b("lnw3")
        e1 = EPS_OUT if fused else (EPS_OUT if gcol is None else EPS_IN)
        self.dve("tensor_scalar", self.lnw0[:], varv, e1, None, ALU.add,
                 reads=[sb_], writes=[w0])
        self.act(self.lnw1[:], self.lnw0[:], AF.Ln, reads=[w0], writes=[w1])
        if fused:
            self.dve("reciprocal", self.lnw2[:], self.lnw0[:], reads=[w0], writes=[w2])
            self.dve("tensor_tensor", self.lnw3[:], varv, self.lnw2[:], ALU.mult,
                     reads=[sb_, w2], writes=[w3])
            self.dve("tensor_scalar", self.lnw3[:], self.lnw3[:], EPS_IN, None, ALU.add,
                     reads=[w3], writes=[w3])
            self.act(self.lnw2[:], self.lnw1[:], AF.Exp, scale=-0.5, reads=[w1], writes=[w2])
            self.act(self.lnw0[:], self.lnw3[:], AF.Ln, reads=[w3], writes=[w0])
            self.dve("tensor_tensor", self.lnw1[:], self.lnw1[:], self.lnw0[:], ALU.add,
                     reads=[w1, w0], writes=[w1])
        self.act(self.scomb[:], self.lnw1[:], AF.Exp, scale=-0.5, reads=[w1], writes=[ssb])
        PAb = b("PA")
        for tt in range(NTT):
            xt, xb = xts[tt % 3], xbs[tt % 3]
            rd = [src_buf] if src_buf else []
            kb.dma(xt[:], src_ap[tt * 128:(tt + 1) * 128, :], reads=rd, writes=[xb])
            zr = (self.zr0, self.zr1)[tt % 2]
            zb = (b("zr0"), b("zr1"))[tt % 2]
            mcol = self.sap(self.stats, 2 * tt, [[1, 1]])
            scol = self.scomb[:, tt:tt + 1]
            self.dve("tensor_scalar", zr[:], xt[:], mcol, scol, ALU.subtract, ALU.mult,
                     reads=[xb, sb_, ssb], writes=[zb])
            if fused:
                ott = (self.ot0, self.ot1)[tt % 2]
                otb = (b("ot0"), b("ot1"))[tt % 2]
                self.dve("tensor_scalar", ott[:], xt[:], mcol,
                         self.lnw2[:, tt:tt + 1], ALU.subtract, ALU.mult,
                         reads=[xb, sb_, w2], writes=[otb])
                kb.dma(self.d_aS[tt * 128:(tt + 1) * 128, :], ott[:],
                       reads=[otb], writes=[b("d_aS")])
            if colsum:
                self.mm(self.pg[0:1, 0:512], self.ones_col[:], zr[:, 0:512],
                        reads=[b("ones_col"), zb],
                        writes=[b("pg")] if tt in (0, NTT - 1) else (),
                        start=(tt == 0), stop=(tt == NTT - 1))
                self.mm(self.pc0[0:1, 0:512], self.ones_col[:], zr[:, 512:1024],
                        reads=[b("ones_col"), zb],
                        writes=[b("pc0")] if tt in (0, NTT - 1) else (),
                        start=(tt == 0), stop=(tt == NTT - 1))
            fns = []
            for dt in range(8):
                fns.append(lambda h, o=self.ptr[:, dt * 128:(dt + 1) * 128],
                           i=zr[:, dt * 128:(dt + 1) * 128], e=self.sident[:]:
                           h.transpose(o, i, e))
            self.mm_group(fns, reads=[zb, b("sident")], writes=[b("ptr")])
            for dt in range(8):
                dst = self.PA[:, dt * T + tt * 128: dt * T + (tt + 1) * 128]
                srcp = self.ptr[:, dt * 128:(dt + 1) * 128]
                if gcol is not None:
                    self.dve("tensor_scalar", dst, srcp, gcol[:, dt:dt + 1], None,
                             ALU.mult, reads=[b("ptr"), b(gcol is self.g1c and "g1c" or "g2c")],
                             writes=[PAb])
                else:
                    self.dve("tensor_copy", dst, srcp, reads=[b("ptr")], writes=[PAb])
        if colsum:
            self.dve("tensor_copy", self.csum[0:1, 0:512], self.pg[0:1, 0:512],
                     reads=[b("pg")], writes=[b("csum")])
            self.dve("tensor_copy", self.csum[0:1, 512:1024], self.pc0[0:1, 0:512],
                     reads=[b("pc0")], writes=[b("csum")])
            self.dve("tensor_copy", self.csbf[0:1, 0:1024], self.csum[0:1, 0:1024], reads=[b("csum")], writes=[b("csbf")])
            fns = []
            for dt in range(8):
                fns.append(lambda h, o=self.ptr[:, 2 * dt:2 * dt + 1],
                           i=self.csbf[0:1, dt * 128:(dt + 1) * 128],
                           e=self.sident[64:65, 64:65]:
                           h.transpose(o, i, e))
            self.mm_group(fns, reads=[b("csbf"), b("sident")], writes=[b("ptr")])
            self.dve("tensor_copy", self.csT[:], self.sap(self.ptr, 0, [[2, 8]]),
                     reads=[b("ptr")], writes=[b("csT")])

    def emit_meancol(self, gcol, bcol, gname, bname):
        b = self.buf
        self.dve("tensor_scalar", self.mnc[:], self.csT[:], 1.0 / T, None, ALU.mult,
                 reads=[b("csT")], writes=[b("mnc")])
        self.dve("tensor_tensor", self.mnc[:], self.mnc[:], gcol[:], ALU.mult,
                 reads=[b("mnc"), b(gname)], writes=[b("mnc")])
        self.dve("tensor_tensor", self.mnc[:], self.mnc[:], bcol[:], ALU.add,
                 reads=[b("mnc"), b(bname)], writes=[b("mnc")])
        self.dve("tensor_copy", self.mncb[:], self.mnc[:], reads=[b("mnc")], writes=[b("mncb")])

    def emit_router(self, nr, rwb, rwb_name, rb_row, rb_name):
        b = self.buf
        fns = []
        for dt in range(8):
            fns.append(lambda h, o=self.pg[0:1, 0:nr], l=self.mncb[:, dt:dt + 1],
                       r=rwb[:, dt * nr:(dt + 1) * nr], st=(dt == 0), sp=(dt == 7):
                       h.matmul(o, l, r, start=st, stop=sp))
        self.mm_group(fns, reads=[b("mncb"), b(rwb_name)], writes=[b("pg")])
        self.dve("tensor_tensor", self.rrow[0:1, 0:nr], self.pg[0:1, 0:nr], rb_row,
                 ALU.add, reads=[b("pg"), b(rb_name)], writes=[b("rrow")])

    def emit_probs2(self):
        b = self.buf
        self.act(self.t0[0:1, 0:2], self.rrow[0:1, 0:2], AF.Exp,
                 reads=[b("rrow")], writes=[b("t0")])
        self.dve("reduce_sum", self.t3[:], self.t0[0:1, 0:2], axis=AX.X,
                 reads=[b("t0")], writes=[b("t3")])
        self.dve("reciprocal", self.t4[:], self.t3[:], reads=[b("t3")], writes=[b("t4")])
        self.dve("tensor_scalar", self.probs[:], self.t0[0:1, 0:2], self.t4[:], None,
                 ALU.mult, reads=[b("t0"), b("t4")], writes=[b("probs")])

    def emit_topk(self):
        b = self.buf
        r = self.rrow[0:1, 0:NE]
        self.dve("reduce_max", self.t3[:], r, axis=AX.X, reads=[b("rrow")], writes=[b("t3")])
        self.dve("tensor_scalar", self.t0[0:1, 0:NE], r, self.t3[:], None, ALU.is_ge,
                 reads=[b("rrow"), b("t3")], writes=[b("t0")])
        self.dve("tensor_scalar", self.t1[0:1, 0:NE], self.t0[0:1, 0:NE], 1e30, None,
                 ALU.mult, reads=[b("t0")], writes=[b("t1")])
        self.dve("tensor_tensor", self.t2[0:1, 0:NE], r, self.t1[0:1, 0:NE],
                 ALU.subtract, reads=[b("rrow"), b("t1")], writes=[b("t2")])
        self.dve("reduce_max", self.t4[:], self.t2[0:1, 0:NE], axis=AX.X,
                 reads=[b("t2")], writes=[b("t4")])
        self.dve("tensor_scalar", self.t1[0:1, 0:NE], self.t2[0:1, 0:NE], self.t4[:],
                 None, ALU.is_ge, reads=[b("t2"), b("t4")], writes=[b("t1")])
        self.dve("tensor_tensor", self.t2[0:1, 0:NE], self.t0[0:1, 0:NE], self.io8r[:],
                 ALU.mult, reads=[b("t0"), b("io8r")], writes=[b("t2")])
        self.dve("reduce_sum", self.eidx[0:1, 0:1], self.t2[0:1, 0:NE], axis=AX.X,
                 reads=[b("t2")], writes=[b("eidx")])
        self.dve("tensor_tensor", self.t2[0:1, 0:NE], self.t1[0:1, 0:NE], self.io8r[:],
                 ALU.mult, reads=[b("t1"), b("io8r")], writes=[b("t2")])
        self.dve("reduce_sum", self.eidx[0:1, 1:2], self.t2[0:1, 0:NE], axis=AX.X,
                 reads=[b("t2")], writes=[b("eidx")])
        self.dve("tensor_tensor", self.t5[:], self.t4[:], self.t3[:], ALU.subtract,
                 reads=[b("t4"), b("t3")], writes=[b("t5")])
        self.act(self.t6[:], self.t5[:], AF.Exp, reads=[b("t5")], writes=[b("t6")])
        self.dve("tensor_scalar", self.t7[:], self.t6[:], 1.0, None, ALU.add,
                 reads=[b("t6")], writes=[b("t7")])
        self.dve("reciprocal", self.t3[:], self.t7[:], reads=[b("t7")], writes=[b("t3")])
        self.dve("tensor_copy", self.epv[0:1, 0:1], self.t3[:], reads=[b("t3")], writes=[b("epv")])
        self.dve("tensor_tensor", self.epv[0:1, 1:2], self.t6[:], self.t3[:], ALU.mult,
                 reads=[b("t6"), b("t3")], writes=[b("epv")])

    def emit_attention(self, hp, ksrc, qsrc, scale, s_count_fn, mask_last4,
                       pm_ap, pm_b, ctx_off, gidx0):
        kb, b = self.kb, self.buf
        self.ps_mode("full")
        gi = gidx0
        (kh0, kl0, kh0hi, kc0, kb0), (kh1, kl1, kh1hi, kc1, kb1) = ksrc
        (qh0, ql0, qh0hi, qc0, qb0), (qh1, ql1, qh1hi, qc1, qb1) = qsrc
        for qch in range(NCH):
            ns = s_count_fn(qch)

            def emit_ctx(si, wT, wb, ns=ns):
                vbase = 2 * T + si * 130
                for h2 in range(2):
                    pc = (self.pc0, self.pc1)[h2]
                    pcb = (b("pc0"), b("pc1"))[h2]
                    vsl = self.PD[0:128, vbase + h2 * 65: vbase + h2 * 65 + 65]
                    wr = [pcb] if (si == 0 or si == ns - 1) else ()
                    self.mm(pc[0:65, 0:512], vsl, wT[:, h2 * 512:(h2 + 1) * 512],
                            reads=[b("PDv"), wb], writes=wr,
                            start=(si == 0), stop=(si == ns - 1))

            pend = []
            for si in range(ns):
                ps = (self.psA, self.psB)[gi % 2]
                psb = (b("psA"), b("psB"))[gi % 2]
                st = si * 128
                l0 = kh0[kl0:kh0hi, kc0 + st: kc0 + st + 128]
                r0 = qh0[ql0:qh0hi, qc0 + qch * 512: qc0 + (qch + 1) * 512]
                l1 = kh1[kl1:kh1hi, kc1 + st: kc1 + st + 128]
                r1 = qh1[ql1:qh1hi, qc1 + qch * 512: qc1 + (qch + 1) * 512]
                fns = [
                    lambda h, o=ps[:, 0:512], l=l0, r=r0: h.matmul(o, l, r, start=True, stop=True),
                    lambda h, o=ps[:, 512:1024], l=l1, r=r1: h.matmul(o, l, r, start=True, stop=True),
                ]
                self.mm_group(fns, reads=[kb0, kb1, qb0, qb1], writes=[psb])
                wT = (self.wT0, self.wT1, self.wT2)[gi % 3]
                wb = (b("wT0"), b("wT1"), b("wT2"))[gi % 3]
                self.act(wT[:], ps[:], AF.Exp, scale=scale, reads=[psb], writes=[wb])
                if mask_last4 and si >= ns - 4:
                    k = si - (ns - 4)
                    mk = self.dmk[:, (3 - k) * 128:(3 - k) * 128 + 512]
                    self.dve("tensor_tensor", wT[:, 0:512], wT[:, 0:512], mk,
                             ALU.mult, reads=[wb, b("dmk")], writes=[wb])
                    self.dve("tensor_tensor", wT[:, 512:1024], wT[:, 512:1024], mk,
                             ALU.mult, reads=[wb, b("dmk")], writes=[wb])
                pend.append((si, wT, wb))
                if len(pend) > 2:
                    emit_ctx(*pend.pop(0))
                gi += 1
            for pp in pend:
                emit_ctx(*pp)
            hold = []
            for h2 in range(2):
                pc = (self.pc0, self.pc1)[h2]
                pcb = (b("pc0"), b("pc1"))[h2]
                rz = (self.rz0, self.rz1)[h2]
                rzb = (b("rz0"), b("rz1"))[h2]
                slot = (qch * 2 + h2) % 2
                rbt = (self.rb0, self.rb1)[slot]
                rbb = (b("rb0"), b("rb1"))[slot]
                craw = (self.craw0, self.craw1)[h2]
                crb = (b("craw0"), b("craw1"))[h2]
                self.dve("reciprocal", rz[0:1, 0:512], pc[64:65, 0:512], reads=[pcb], writes=[rzb])
                if pm_ap is not None:
                    self.dve("tensor_scalar", rz[0:1, 0:512], rz[0:1, 0:512], pm_ap,
                             None, ALU.mult, reads=[rzb, pm_b], writes=[rzb])
                self.dve("tensor_copy", craw[0:64, :], pc[0:64, 0:512],
                         reads=[pcb], writes=[crb])
                dst = self.PB[h2 * 64:(h2 + 1) * 64,
                              ctx_off + qch * 512: ctx_off + (qch + 1) * 512]
                if _NOBC:
                    self.dve("tensor_copy", dst, craw[0:64, :],
                             reads=[crb], writes=[b("PB")])
                else:
                    self.bcast_row(rz[0:1, 0:512], [rzb], rbt[:], rbb, slot, 64, 512)
                    hold.append((dst, craw, crb, rbt, rbb))
            for (dst, craw, crb, rbt, rbb) in hold:
                self.dve("tensor_tensor", dst, craw[0:64, :], rbt[:], ALU.mult,
                         reads=[crb, rbb], writes=[b("PB")])
        return gi

    def emit_gen_ffn(self, m, hp):
        kb, b = self.kb, self.buf
        wqv = self.PC[:, 0:1024].rearrange("p (dt c) -> p dt c", dt=8)
        wkvv = self.PC[:, 1024:2048].rearrange("p (dt c) -> p dt c", dt=8)
        src_q = self.d_wq[m, :, hp * 128:(hp + 1) * 128].rearrange("(dt p) c -> p dt c", p=128)
        src_k = self.d_wk[m, :, hp * 128:(hp + 1) * 128].rearrange("(dt p) c -> p dt c", p=128)
        kb.pool_dma(wqv, src_q, writes=[b("PCq")])
        kb.pool_dma(wkvv, src_k, writes=[b("PCk")])
        wvv = self.PC[:, 2048:3072].rearrange("p (dt c) -> p dt c", dt=8)
        src_v = self.d_wv[m, :, hp * 128:(hp + 1) * 128].rearrange("(dt p) c -> p dt c", p=128)
        kb.pool_dma(wvv, src_v, writes=[b("PCv")])
        if _CUT == f"g{hp}load" and m == 0:
            for (ofs, bn) in ((0, "PCq"), (1024, "PCk"), (2048, "PCv")):
                self.dve("tensor_copy", self.VBF[0:1, 0:128],
                         self.PC[0:1, ofs:ofs + 128], reads=[b(bn)], writes=[b("VBF")])
            return True
        for (wofs, wbuf, bio, bname, bc, bcn) in (
                (0, b("PCq"), self.biasq, "biasq", self.bqcs, "bqcs"),
                (1024, b("PCk"), self.biask, "biask", self.bkcs, "bkcs")):
            fns = []
            n = 0
            for dt in range(8):
                for (hl, hlb) in ((self.b1h, "b1h"), (self.b1l, "b1l")):
                    fns.append(lambda h, o=self.pg[0:128, 0:1],
                               l=self.PC[:, wofs + dt * 128: wofs + (dt + 1) * 128],
                               r=hl[:, dt:dt + 1], st=(n == 0), sp=(n == 15):
                               h.matmul(o, l, r, start=st, stop=sp))
                    n += 1
            self.mm_group(fns, reads=[wbuf, b("b1h"), b("b1l")], writes=[b("pg")])
            self.dve("tensor_scalar", bio[:], self.pg[0:128, 0:1],
                     bc[:, m * 8 + hp: m * 8 + hp + 1], None, ALU.add,
                     reads=[b("pg"), b(bcn)], writes=[b(bname)])
        fns = []
        n = 0
        for dt in range(8):
            for hl in (self.b1h, self.b1l):
                fns.append(lambda h, o=self.pg[0:1, 0:128], l=hl[:, dt:dt + 1],
                           r=self.PC[:, 2048 + dt * 128: 2048 + (dt + 1) * 128],
                           st=(n == 0), sp=(n == 15):
                           h.matmul(o, l, r, start=st, stop=sp))
                n += 1
        self.mm_group(fns, reads=[b("PCv"), b("b1h"), b("b1l")], writes=[b("pg")])
        self.dve("tensor_tensor", self.VBF[0:1, 0:128], self.pg[0:1, 0:128],
                 self.bvr[0:1, m * 1024 + hp * 128: m * 1024 + (hp + 1) * 128],
                 ALU.add, reads=[b("pg"), b("bvr")], writes=[b("VBF")])
        self.dve("tensor_copy", self.VBR[0:1, 0:128], self.VBF[0:1, 0:128],
                 reads=[b("VBF")], writes=[b("VBR")])
        self.dve("tensor_tensor", self.VBF[0:1, 128:256], self.VBF[0:1, 0:128],
                 self.VBR[0:1, 0:128], ALU.subtract, reads=[b("VBF"), b("VBR")],
                 writes=[b("VBF")])
        self.dve("tensor_copy", self.VBR[0:1, 128:256], self.VBF[0:1, 128:256],
                 reads=[b("VBF")], writes=[b("VBR")])
        if _CUT == f"g{hp}bias" and m == 0:
            return True
        self.ps_mode("half")
        ri = 0
        for (wofs, wbuf, dbase, dstb, bio, bname) in (
                (0, b("PCq"), 0, b("PDq"), self.biasq, "biasq"),
                (1024, b("PCk"), T, b("PDk"), self.biask, "biask")):
            for tch in range(8):
                pt, ptb = self.ps_half(ri)
                ri += 1
                fns = []
                for dt in range(8):
                    fns.append(lambda h, o=pt,
                               l=self.PC[:, wofs + dt * 128: wofs + (dt + 1) * 128],
                               r=self.PA[:, dt * T + tch * 512: dt * T + (tch + 1) * 512],
                               st=(dt == 0), sp=(dt == 7):
                               h.matmul(o, l, r, start=st, stop=sp))
                self.mm_group(fns, reads=[wbuf, b("PA")], writes=[ptb])
                self.dve("tensor_scalar",
                         self.PD[:, dbase + tch * 512: dbase + (tch + 1) * 512],
                         pt, bio[:], None, ALU.add,
                         reads=[ptb, b(bname)], writes=[dstb])
        if _CUT == f"g{hp}qk" and m == 0:
            self.ps_mode("full")
            return True
        for tt in range(NTT):
            pt, ptb = self.ps_half(ri, w=128)
            ri += 1
            fns = []
            for dt in range(8):
                fns.append(lambda h, o=pt,
                           l=self.PA[:, dt * T + tt * 128: dt * T + (tt + 1) * 128],
                           r=self.PC[:, 2048 + dt * 128: 2048 + (dt + 1) * 128],
                           st=(dt == 0), sp=False:
                           h.matmul(o, l, r, start=st, stop=sp))
            if _NOVB:
                fns[-1] = (lambda h, o=pt,
                           l=self.PA[:, 7 * T + tt * 128: 7 * T + (tt + 1) * 128],
                           r=self.PC[:, 2048 + 7 * 128: 2048 + 8 * 128]:
                           h.matmul(o, l, r, start=False, stop=True))
            else:
                for (c0, last) in ((0, False), (128, True)):
                    fns.append(lambda h, o=pt,
                               l=self.onesR[0:1, :],
                               r=self.VBR[0:1, c0:c0 + 128], sp=last:
                               h.matmul(o, l, r, start=False, stop=sp))
            self.mm_group(fns, reads=[b("PA"), b("PCv"), b("onesR"),
                                      b("VBR")], writes=[ptb])
            dstv = self.sap(self.PD, 2 * T + tt * 130, [[65, 2], [1, 64]])
            self.dve("tensor_copy", dstv,
                     pt.rearrange("p (g c) -> p g c", c=64),
                     reads=[ptb], writes=[b("PDv")])
        self.ps_mode("full")

    def emit_proj_pass(self, wo_src_ap, half, src_ap, src_buf, dst_ap, dst_buf,
                      bias_rows, with_stats):
        kb, b = self.kb, self.buf
        wov = self.WOH[:].rearrange("p (hh d) -> p hh d", hh=4)
        kb.pool_dma(wov, wo_src_ap, writes=[b("WOH")])
        if _CUT == "mha_w0":
            return
        self.ps_mode("half")
        for tt in range(NTT):
            xt, xb = (self.xt0, self.xt1)[tt % 2], (b("xt0"), b("xt1"))[tt % 2]
            rd = [src_buf] if src_buf else []
            kb.dma(xt[:], src_ap[tt * 128:(tt + 1) * 128, :], reads=rd, writes=[xb])
            ot, ob = (self.ot0, self.ot1)[tt % 2], (b("ot0"), b("ot1"))[tt % 2]
            for dch in range(2):
                pst, psb = self.ps_half(tt * 2 + dch)
                fns = []
                for hh in range(4):
                    fns.append(lambda h, o=pst,
                               l=self.PB[:, hh * T + tt * 128: hh * T + (tt + 1) * 128],
                               r=self.WOH[:, hh * 1024 + dch * 512: hh * 1024 + (dch + 1) * 512],
                               st=(hh == 0), sp=(hh == 3 and bias_rows is None):
                               h.matmul(o, l, r, start=st, stop=sp))
                rds = [b("PB"), b("WOH")]
                if bias_rows is not None:
                    for j, (br, brn, obase) in enumerate(bias_rows):
                        fns.append(lambda h, o=pst,
                                   l=self.onesR[obase:obase + 1, :],
                                   r=br[0:1, dch * 512:(dch + 1) * 512], sp=(j == 1):
                                   h.matmul(o, l, r, start=False, stop=sp))
                        rds.append(b(brn))
                    rds.append(b("onesR"))
                self.mm_group(fns, reads=rds, writes=[psb])
                self.dve("tensor_tensor", ot[:, dch * 512:(dch + 1) * 512], pst,
                         xt[:, dch * 512:(dch + 1) * 512], ALU.add,
                         reads=[psb, xb], writes=[ob])
            if with_stats:
                st6 = (self.st6a, self.st6b)[tt % 2]
                s6b = (b("st6a"), b("st6b"))[tt % 2]
                self.dve("bn_stats", st6[:, 0:6], ot[:, 0:512], reads=[ob], writes=[s6b])
                self.dve("bn_stats", st6[:, 6:12], ot[:, 512:1024], reads=[ob], writes=[s6b])
                self.dve("bn_aggr", self.stats[:, 2 * tt:2 * tt + 2], st6[:, 0:12],
                         reads=[s6b], writes=[b("stats")])
            kb.dma(dst_ap[tt * 128:(tt + 1) * 128, :], ot[:], reads=[ob],
                   writes=[dst_buf], q="sc")
        self.ps_mode("full")

    def emit_expert_select_gather(self, slot):
        kb, b = self.kb, self.buf
        bcc = (self.bcc0, self.bcc1)[slot]
        bccb = (b("bcc0"), b("bcc1"))[slot]
        bcc2 = (self.bcc2, self.bcc3)[slot]
        bcc2b = (b("bcc2"), b("bcc3"))[slot]
        self.dve("tensor_scalar", self.t5[:], self.eidx[0:1, slot:slot + 1], 1024.0,
                 None, ALU.mult, reads=[b("eidx")], writes=[b("t5")])
        self.bcast_row(self.t5[:], [b("t5")], bcc[:], bccb, 4 + slot, 128, 1)
        self.dve("tensor_scalar", self.iif1[:], self.io8f[:], bcc[:], None, ALU.add,
                 reads=[b("io8f"), bccb], writes=[b("iif1")])
        self.dve("tensor_copy", self.ii1[:], self.iif1[:], reads=[b("iif1")], writes=[b("ii1")])
        self.dve("tensor_scalar", self.t5[:], self.eidx[0:1, slot:slot + 1], 2048.0,
                 None, ALU.mult, reads=[b("eidx")], writes=[b("t5")])
        self.bcast_row(self.t5[:], [b("t5")], bcc2[:], bcc2b, 6 + slot, 128, 1)
        self.dve("tensor_scalar", self.iif2[:], self.io16f[:], bcc2[:], None, ALU.add,
                 reads=[b("io16f"), bcc2b], writes=[b("iif2")])
        self.dve("tensor_copy", self.ii2[:], self.iif2[:], reads=[b("iif2")], writes=[b("ii2")])
        for dt in range(8):
            kb.pool_dma(self.PB[:, dt * FF:(dt + 1) * FF], self.d_fw1,
                        reads=[b("ii1")], writes=[b("PB")],
                        indirect=bass.IndirectOffsetOnAxis(self.ii1[:, dt:dt + 1], 0))
        for ft in range(16):
            if ft < 12:
                dst = self.PD[:, ft * 1024:(ft + 1) * 1024]
                dbf = b("fw2v")
            else:
                dst = self.qcatA[:, (ft - 12) * 1024:(ft - 11) * 1024]
                dbf = b("qcA")
            kb.pool_dma(dst, self.d_fw2, reads=[b("ii2")], writes=[dbf],
                        indirect=bass.IndirectOffsetOnAxis(self.ii2[:, ft:ft + 1], 0))

    def emit_expert_compute(self, slot, ci, xsrc_ap, xsrc_buf, dst_ap, dst_buf):
        kb, b = self.kb, self.buf
        for ft in range(16):
            fns = []
            n = 0
            for dt in range(8):
                for hl in (self.b2h, self.b2l):
                    fns.append(lambda h, o=self.pg[0:128, 0:1],
                               l=self.PB[:, dt * FF + ft * 128: dt * FF + (ft + 1) * 128],
                               r=hl[:, dt:dt + 1], st=(n == 0), sp=(n == 15):
                               h.matmul(o, l, r, start=st, stop=sp))
                    n += 1
            self.mm_group(fns, reads=[b("PB"), b("b2h"), b("b2l")], writes=[b("pg")])
            self.dve("tensor_copy", self.hbias[:, ft:ft + 1], self.pg[0:128, 0:1],
                     reads=[b("pg")], writes=[b("hbias")])
        pcol = (self.p1c, self.p2c)[slot]
        pcolb = (b("p1c"), b("p2c"))[slot]
        self.ps_mode("half")
        for tch in range(8):
            for ft in range(16):
                pst, psb = self.ps_half(ft)
                fns = []
                for dt in range(8):
                    fns.append(lambda h, o=pst,
                               l=self.PB[:, dt * FF + ft * 128: dt * FF + (ft + 1) * 128],
                               r=self.PA[:, dt * T + tch * 512: dt * T + (tch + 1) * 512],
                               st=(dt == 0), sp=(dt == 7):
                               h.matmul(o, l, r, start=st, stop=sp))
                self.mm_group(fns, reads=[b("PB"), b("PA")], writes=[psb])
                if ft < 8:
                    hdst, hb = self.WOH[:, ft * 512:(ft + 1) * 512], b("WOH")
                else:
                    hdst, hb = self.PC[:, (ft - 8) * 512:(ft - 7) * 512], b("hTb")
                self.act(hdst, pst, AF.Gelu, bias=self.hbias[:, ft:ft + 1],
                         reads=[psb, b("hbias")], writes=[hb])
            for tl in range(4):
                tt = tch * 4 + tl
                in1, in1b = (self.xt0, self.xt1)[tt % 2], (b("xt0"), b("xt1"))[tt % 2]
                in2, in2b = self.xt2, b("xt2")
                ot, ob = (self.ot0, self.ot1)[tt % 2], (b("ot0"), b("ot1"))[tt % 2]
                rows = slice(tt * 128, (tt + 1) * 128)
                if slot == 0:
                    kb.dma(in1[:], xsrc_ap[rows, :], reads=[xsrc_buf] if xsrc_buf else [],
                           writes=[in1b])
                    kb.dma(in2[:], self.d_sB[rows, :], reads=[b("d_sB")], writes=[in2b])
                else:
                    kb.dma(in1[:], self.d_sA[rows, :], reads=[b("d_sA")], writes=[in1b])
                    kb.dma(in2[:], self.d_ya[rows, :], reads=[b("d_ya")], writes=[in2b])
                for dch in range(2):
                    pst, psb = self.ps_half(4 + tl * 2 + dch)
                    fns = []
                    rds = [b("WOH"), b("hTb"), b("fw2v"), b("qcA")]
                    for ft in range(16):
                        if ft < 8:
                            lsl = self.WOH[:, ft * 512 + tl * 128: ft * 512 + (tl + 1) * 128]
                        else:
                            lsl = self.PC[:, (ft - 8) * 512 + tl * 128: (ft - 8) * 512 + (tl + 1) * 128]
                        if ft < 12:
                            rsl = self.PD[:, ft * 1024 + dch * 512: ft * 1024 + (dch + 1) * 512]
                        else:
                            rsl = self.qcatA[:, (ft - 12) * 1024 + dch * 512:
                                             (ft - 12) * 1024 + (dch + 1) * 512]
                        fns.append(lambda h, o=pst, l=lsl, r=rsl,
                                   st=(ft == 0), sp=(ft == 15):
                                   h.matmul(o, l, r, start=st, stop=sp))
                    self.mm_group(fns, reads=rds, writes=[psb])
                    self.dve("tensor_scalar", ot[:, dch * 512:(dch + 1) * 512], pst,
                             pcol[:], None, ALU.mult, reads=[psb, pcolb], writes=[ob])
                if slot == 0:
                    kb.dma(self.d_ya[rows, :], ot[:], reads=[ob], writes=[b("d_ya")])
                    self.dve("tensor_scalar", in2[:], in2[:], 0.5, None, ALU.mult,
                             reads=[in2b], writes=[in2b])
                    self.dve("tensor_tensor", in2[:], in2[:], in1[:], ALU.add,
                             reads=[in2b, in1b], writes=[in2b])
                    kb.dma(self.d_sA[rows, :], in2[:], reads=[in2b], writes=[b("d_sA")])
                else:
                    self.dve("tensor_tensor", ot[:], ot[:], in2[:], ALU.add,
                             reads=[ob, in2b], writes=[ob])
                    self.dve("tensor_tensor", ot[:], ot[:], in1[:], ALU.add,
                             reads=[ob, in1b], writes=[ob])
                    kb.dma(dst_ap[rows, :], ot[:], reads=[ob], writes=[dst_buf])
        self.ps_mode("full")

    def emit_mha_call(self, ci, xsrc_ap, xsrc_buf, dst_ap, dst_buf):
        kb, b = self.kb, self.buf
        gi = 0
        for m in range(M):
            self.dve("tensor_scalar", self.pbo[0:1, 0:1024],
                     self.bor[0:1, m * D:(m + 1) * D],
                     self.probs[0:1, m:m + 1], None, ALU.mult,
                     reads=[b("bor"), b("probs")], writes=[b("pbo")])
            self.dve("tensor_copy", self.pboh[0:1, 0:1024], self.pbo[0:1, 0:1024],
                     reads=[b("pbo")], writes=[b("pboh")])
            self.dve("tensor_tensor", self.pbol[0:1, 0:1024], self.pbo[0:1, 0:1024],
                     self.pboh[0:1, 0:1024], ALU.subtract,
                     reads=[b("pbo"), b("pboh")], writes=[b("pbol")])
            for half in range(2):
                for hp in range(half * 4, half * 4 + 4):
                    if self.emit_gen_ffn(m, hp):
                        return
                    if _CUT == "mha_g0":
                        return
                    if _CUT == "mha_g2" and hp == 2:
                        return
                    if _CUT == "mha_g3" and hp == 3:
                        return
                    ksrc = ((self.PD, 0, 64, T, b("PDk")), (self.PD, 64, 128, T, b("PDk")))
                    qsrc = ((self.PD, 0, 64, 0, b("PDq")), (self.PD, 64, 128, 0, b("PDq")))
                    gi = self.emit_attention(
                        hp, ksrc, qsrc, SC_FFN, lambda qch: 32, False,
                        self.probs[0:1, m:m + 1], b("probs"), (hp % 4) * T, gi)
                    if _CUT == "mha_a0":
                        return
                    if _CUT == "mha_a1" and hp == 1:
                        return
                    if _CUT == "mha_a2" and hp == 2:
                        return
                if _CUT == "mha_h0":
                    return
                if _CUT == "mha_h1" and half == 1:
                    return
                if _CUT == "mha_h2" and m == 1:
                    return
                srcs = {(0, 0): (self.d_aS, b("d_aS")), (0, 1): (self.d_sA, b("d_sA")),
                        (1, 0): (self.d_sB, b("d_sB")), (1, 1): (self.d_sA, b("d_sA"))}
                dsts = {(0, 0): (self.d_sA, b("d_sA")), (0, 1): (self.d_sB, b("d_sB")),
                        (1, 0): (self.d_sA, b("d_sA")), (1, 1): (self.d_sB, b("d_sB"))}
                sap, sbuf_ = srcs[(m, half)]
                dap, dbuf_ = dsts[(m, half)]
                wo_src = self.d_wo[m, half * 512:(half + 1) * 512, :].rearrange(
                    "(hh p) d -> p hh d", p=128)
                bias_rows = [(self.pboh, "pboh", 0), (self.pbol, "pbol", 0)] if half == 0 else None
                self.emit_proj_pass(wo_src, half, sap, sbuf_, dap, dbuf_,
                                    bias_rows, with_stats=(m == 1 and half == 1))
                if _CUT in ("mha_p0", "mha_w0"):
                    return
                if _CUT == "mha_p1" and (m, half) == (0, 1):
                    return
                if _CUT == "mha_p2" and (m, half) == (1, 0):
                    return
        if _CUT == f"mha{ci}_mo":
            self.emit_taps()
            return
        self.emit_ln(self.d_sB, b("d_sB"), self.g2c, fused=False, colsum=True,
                     stats_ready=True)
        self.emit_meancol(self.g2c, self.b2c, "g2c", "b2c")
        self.emit_router(NE, self.frwb, "frwb", self.frbr[0:1, 0:NE], "frbr")
        self.emit_topk()
        self.dve("tensor_scalar", self.t5[:], self.epv[0:1, 0:1], 0.5, None, ALU.mult,
                 reads=[b("epv")], writes=[b("t5")])
        self.bcast_row(self.t5[:], [b("t5")], self.p1c[:], b("p1c"), 4, 128, 1)
        self.dve("tensor_scalar", self.t6[:], self.epv[0:1, 1:2], 0.5, None, ALU.mult,
                 reads=[b("epv")], writes=[b("t6")])
        self.bcast_row(self.t6[:], [b("t6")], self.p2c[:], b("p2c"), 5, 128, 1)
        if _CUT == f"mha{ci}_router2":
            self.emit_taps()
            return
        kb.alias(b("fw2v"), [b("PDq"), b("PDk"), b("PDv")])
        kb.alias(b("qcA"), [b("qcA")])
        kb.alias(b("hTb"), [b("PCq"), b("PCk"), b("PCv")])
        self.emit_expert_select_gather(0)
        self.emit_expert_compute(0, ci, xsrc_ap, xsrc_buf, None, None)
        self.emit_expert_select_gather(1)
        self.emit_expert_compute(1, ci, xsrc_ap, xsrc_buf, dst_ap, dst_buf)
        kb.alias(b("PDq"), [b("fw2v")])
        kb.alias(b("PDk"), [b("fw2v")])
        kb.alias(b("PDv"), [b("fw2v")])
        for nm in ("PCq", "PCk", "PCv"):
            kb.alias(b(nm), [b("hTb")])
        vp_ap = self.sap(self.PD, 2 * T + 64, [[130, NTT], [65, 2]])
        self.dve("memset", vp_ap, 1.0, writes=[b("PDv")])

    def emit_mla(self):
        kb, b = self.kb, self.buf
        self.emit_ln(self.d_xs1, b("d_xs1"), None, fused=False, colsum=False)
        kb.pool_dma(self.PC[:, 0:1024].rearrange("p (dt c) -> p dt c", dt=8),
                    self.d_wkv.rearrange("(dt p) c -> p dt c", p=128), writes=[b("PCq")])
        kb.pool_dma(self.PC[:, 1024:2048], self.d_wkup, writes=[b("PCk")])
        kb.pool_dma(self.PC[:, 2048:3072], self.d_wvup, writes=[b("PCv")])
        kb.pool_dma(self.PC[:, 3072:3120], self.d_wkr, writes=[b("PCv")])
        kb.pool_dma(self.PC[:, 3120:3504].rearrange("p (dt c) -> p dt c", dt=8),
                    self.d_wqr.rearrange("(dt p) c -> p dt c", p=128), writes=[b("PCv")])
        for tch in range(8):
            fns = []
            for dt in range(8):
                fns.append(lambda h, o=self.pg[:, 0:512],
                           l=self.PC[:, dt * 128:(dt + 1) * 128],
                           r=self.PA[:, dt * T + tch * 512: dt * T + (tch + 1) * 512],
                           st=(dt == 0), sp=(dt == 7):
                           h.matmul(o, l, r, start=st, stop=sp))
            self.mm_group(fns, reads=[b("PCq"), b("PA")], writes=[b("pg")])
            self.dve("tensor_copy", self.ckvT[:, tch * 512:(tch + 1) * 512],
                     self.pg[:, 0:512], reads=[b("pg")], writes=[b("ckvT")])
        for (kind, dsts) in (("k", ((self.PD, 0), (self.PD, T))),
                             ("q", ((self.qcatA, 0), (self.WOH, 0)))):
            dbufs = (b("PDq"), b("PDk")) if kind == "k" else (b("qcA"), b("WOH"))
            for tch in range(8):
                cols = slice(tch * 512, (tch + 1) * 512)
                if kind == "k":
                    self.mm(self.pg[0:48, 0:512], self.PC[:, 3072:3120],
                            self.ckvT[:, cols], reads=[b("PCv"), b("ckvT")],
                            writes=[b("pg")])
                else:
                    fns = []
                    for dt in range(8):
                        fns.append(lambda h, o=self.pg[0:48, 0:512],
                                   l=self.PC[:, 3120 + dt * 48: 3120 + (dt + 1) * 48],
                                   r=self.PA[:, dt * T + tch * 512: dt * T + (tch + 1) * 512],
                                   st=(dt == 0), sp=(dt == 7):
                                   h.matmul(o, l, r, start=st, stop=sp))
                    self.mm_group(fns, reads=[b("PCv"), b("PA")], writes=[b("pg")])
                self.dve("tensor_copy", self.xt2[0:48, 0:512], self.pg[0:48, 0:512],
                         reads=[b("pg")], writes=[b("krp")])
                kb.dma(self.cs0[:], self.d_cosT[:, cols], writes=[b("cs0")])
                kb.dma(self.cs32[:], self.d_cosT[:, cols], writes=[b("cs32")])
                kb.dma(self.sn0[:], self.d_sinT[:, cols], writes=[b("sn0")])
                kb.dma(self.sn32[:], self.d_sinT[:, cols], writes=[b("sn32")])
                base = dsts[0][1]
                self.dve("tensor_tensor", self.rt1[:], self.krp1[0:16, :], self.cs0[:],
                         ALU.mult, reads=[b("krp"), b("cs0")], writes=[b("rt1")])
                self.dve("tensor_tensor", self.rt2[:], self.krp2[0:16, :], self.sn32[:],
                         ALU.mult, reads=[b("krp"), b("sn32")], writes=[b("rt2")])
                self.dve("tensor_tensor", dsts[0][0][64:80, base + tch * 512: base + tch * 512 + 512],
                         self.rt1[:], self.rt2[:], ALU.subtract,
                         reads=[b("rt1"), b("rt2")], writes=[dbufs[0]])
                self.dve("tensor_tensor", self.rt1[:], self.krp2[0:16, :], self.cs32[:],
                         ALU.mult, reads=[b("krp"), b("cs32")], writes=[b("rt1")])
                self.dve("tensor_tensor", self.rt2[:], self.krp1[0:16, :], self.sn0[:],
                         ALU.mult, reads=[b("krp"), b("sn0")], writes=[b("rt2")])
                self.dve("tensor_tensor", self.rote[0:16, :], self.rt1[:], self.rt2[:],
                         ALU.add, reads=[b("rt1"), b("rt2")], writes=[b("rote")])
                kb.dma(dsts[0][0][80:96, base + tch * 512: base + tch * 512 + 512],
                       self.rote[0:16, :], reads=[b("rote")], writes=[dbufs[0]])
                b2_, bs2 = dsts[1]
                kb.dma(b2_[64:96, bs2 + tch * 512: bs2 + tch * 512 + 512],
                       dsts[0][0][64:96, base + tch * 512: base + tch * 512 + 512],
                       reads=[dbufs[0]], writes=[dbufs[1]])
        gi = 0
        for half in range(2):
            if half == 1:
                kb.dma(self.WOH[64:96, :], self.qcatA[64:96, :],
                       reads=[b("qcA")], writes=[b("WOH")])
            for hp in range(half * 4, half * 4 + 4):
                kb.pool_dma(self.PC[:, 0:1024].rearrange("p (dt c) -> p dt c", dt=8),
                            self.d_wqa[:, hp * 128:(hp + 1) * 128].rearrange(
                                "(dt p) c -> p dt c", p=128), writes=[b("PCq")])
                self.ps_mode("half")
                rings = ((self.psA, 0, "psA_0"), (self.psA, 512, "psA_1"),
                         (self.psB, 0, "psB_0"), (self.psB, 512, "psB_1"))
                ri = 0
                for tch in range(8):
                    pt, lo, pnm = rings[ri % 4]
                    ri += 1
                    ptb = self.buf(pnm)
                    cols = slice(tch * 512, (tch + 1) * 512)
                    self.mm(pt[:, lo:lo + 512],
                            self.PC[:, 1024 + hp * 128: 1024 + (hp + 1) * 128],
                            self.ckvT[:, cols], reads=[b("PCk"), b("ckvT")], writes=[ptb])
                    self.dve("tensor_copy", self.PD[0:64, tch * 512:(tch + 1) * 512],
                             pt[0:64, lo:lo + 512], reads=[ptb], writes=[b("PDq")])
                    self.dve("tensor_copy", self.PD[0:64, T + tch * 512: T + (tch + 1) * 512],
                             pt[64:128, lo:lo + 512], reads=[ptb], writes=[b("PDk")])
                for tch in range(8):
                    pt, lo, pnm = rings[ri % 4]
                    ri += 1
                    ptb = self.buf(pnm)
                    fns = []
                    for dt in range(8):
                        fns.append(lambda h, o=pt[:, lo:lo + 512],
                                   l=self.PC[:, dt * 128:(dt + 1) * 128],
                                   r=self.PA[:, dt * T + tch * 512: dt * T + (tch + 1) * 512],
                                   st=(dt == 0), sp=(dt == 7):
                                   h.matmul(o, l, r, start=st, stop=sp))
                    self.mm_group(fns, reads=[b("PCq"), b("PA")], writes=[ptb])
                    self.dve("tensor_copy", self.qcatA[0:64, tch * 512:(tch + 1) * 512],
                             pt[0:64, lo:lo + 512], reads=[ptb], writes=[b("qcA")])
                    self.dve("tensor_copy", self.WOH[0:64, tch * 512:(tch + 1) * 512],
                             pt[64:128, lo:lo + 512], reads=[ptb], writes=[b("WOH")])
                for tt in range(NTT):
                    pt, lo, pnm = rings[ri % 4]
                    ri += 1
                    ptb = self.buf(pnm)
                    fns = [lambda h, o=pt[:, lo:lo + 128],
                           l=self.ckvT[:, tt * 128:(tt + 1) * 128],
                           r=self.PC[:, 2048 + hp * 128: 2048 + (hp + 1) * 128]:
                           h.matmul(o, l, r, start=True, stop=True)]
                    self.mm_group(fns, reads=[b("ckvT"), b("PCv")], writes=[ptb])
                    dstv = self.sap(self.PD, 2 * T + tt * 130, [[65, 2], [1, 64]])
                    self.dve("tensor_copy", dstv,
                             pt[:, lo:lo + 128].rearrange("p (g c) -> p g c", c=64),
                             reads=[ptb], writes=[b("PDv")])
                ksrc = ((self.PD, 0, 96, 0, b("PDq")), (self.PD, 0, 96, T, b("PDk")))
                qsrc = ((self.qcatA, 0, 96, 0, b("qcA")), (self.WOH, 0, 96, 0, b("WOH")))
                gi = self.emit_attention(hp, ksrc, qsrc, SC_MLA,
                                         lambda qch: 4 * qch + 4, True,
                                         None, None, (hp % 4) * T, gi)
            wo_src = self.d_woa[half * 512:(half + 1) * 512, :].rearrange(
                "(hh p) d -> p hh d", p=128)
            if half == 0:
                self.emit_proj_pass(wo_src, half, self.d_xs1, b("d_xs1"),
                                    self.d_sA, b("d_sA"), None, False)
            else:
                self.emit_proj_pass(wo_src, half, self.d_sA, b("d_sA"),
                                    self.d_xs2, b("d_xs2"), None, False)

    def emit_taps(self):
        b = self.buf
        if "dbg_PA" in _DBG:
            self.kb.dma(self.d_dbg_PA, self.PA[:], reads=[b("PA")], writes=[b("d_dbg_PA")])
        if "dbg_PB" in _DBG:
            self.kb.dma(self.d_dbg_PB, self.PB[:], reads=[b("PB")], writes=[b("d_dbg_PB")])
        if "dbg_PD" in _DBG:
            self.kb.dma(self.d_dbg_PD, self.PD[:], reads=[b("PDq"), b("PDk"), b("PDv")],
                        writes=[b("d_dbg_PD")])
        if "dbg_qc" in _DBG:
            self.kb.dma(self.d_dbg_qc, self.qcatA[:], reads=[b("qcA")], writes=[b("d_dbg_qc")])
        if "dbg_WOH" in _DBG:
            self.kb.dma(self.d_dbg_WOH, self.WOH[:], reads=[b("WOH")], writes=[b("d_dbg_WOH")])
        if "dbg_ckv" in _DBG:
            self.kb.dma(self.d_dbg_ckv, self.ckvT[:], reads=[b("ckvT")], writes=[b("d_dbg_ckv")])
        if "dbg_row" in _DBG:
            k = self.kb
            k.dma(self.d_dbg_row[:, 0:8], self.csT[:], reads=[b("csT")], writes=[b("d_dbg_row")])
            k.dma(self.d_dbg_row[:, 8:16], self.mnc[:], reads=[b("mnc")], writes=[b("d_dbg_row")])
            k.dma(self.d_dbg_row[0:1, 16:16 + NE], self.rrow[0:1, 0:NE],
                  reads=[b("rrow")], writes=[b("d_dbg_row")])
            k.dma(self.d_dbg_row[0:1, 24:26], self.probs[0:1, 0:2],
                  reads=[b("probs")], writes=[b("d_dbg_row")])
            k.dma(self.d_dbg_row[0:1, 26:28], self.eidx[0:1, 0:2],
                  reads=[b("eidx")], writes=[b("d_dbg_row")])
            k.dma(self.d_dbg_row[0:1, 28:30], self.epv[0:1, 0:2],
                  reads=[b("epv")], writes=[b("d_dbg_row")])
            k.dma(self.d_dbg_row[:, 32:64], self.stats[:, 0:32],
                  reads=[b("stats")], writes=[b("d_dbg_row")])

    def emit(self):
        b = self.buf
        self.emit_setup()
        self.emit_ln(self.d_x0, None, self.g1c, fused=True, colsum=True)
        if _CUT == "ln1":
            self.emit_taps()
            return
        self.emit_meancol(self.g1c, self.b1c, "g1c", "b1c")
        self.emit_router(M, self.mrwb, "mrwb", self.mrbr[0:1, 0:M], "mrbr")
        self.emit_probs2()
        if _CUT == "router1":
            self.emit_taps()
            return
        self.emit_mha_call(0, self.d_x0, None, self.d_xs1, b("d_xs1"))
        if _CUT is not None and _CUT.startswith(("mha_", "mha0", "g2", "g3")):
            self.emit_taps()
            return
        self.emit_mla()
        if _CUT == "mla":
            self.emit_taps()
            return
        self.emit_ln(self.d_xs2, b("d_xs2"), self.g1c, fused=True, colsum=True)
        self.emit_meancol(self.g1c, self.b1c, "g1c", "b1c")
        self.emit_router(M, self.mrwb, "mrwb", self.mrbr[0:1, 0:M], "mrbr")
        self.emit_probs2()
        self.emit_mha_call(1, self.d_xs2, b("d_xs2"), self.d_y, b("d_y"))
        yb = b("d_y")
        for tok in list(yb.w.values()):
            self.kb._wait("sp", tok)


def kernel(**inputs):
    pk = _hostpack(inputs)
    x = np.asarray(inputs["L0_attn_x"], dtype=np.float32)
    nc = build_program()
    in_maps = []
    for c in range(8):
        m = dict(pk)
        m["x0"] = np.ascontiguousarray(x[c >> 1])
        in_maps.append(m)
    res = run_bass_kernel_spmd(nc, in_maps, core_ids=list(range(8)))
    out = np.stack([np.asarray(res.results[2 * bb]["y"], dtype=np.float32)
                    for bb in range(4)], axis=0)
    return out
